# version 06 (final)
# speedup vs baseline: 1.1937x; 1.0058x over previous
import sys
for _p in ("/opt/trn_rl_repo",):
    if _p not in sys.path:
        sys.path.insert(0, _p)
"""Megakernel build library for mk_prb_d2 (shared by all launch programs).

Layout conventions:
- fm (feature-major) activation X: SBUF [128, 8*T] bf16; chunk c (feature rows
  c*128..c*128+127) at free cols [c*T, (c+1)*T). HBM fm tensors are [1024, T].
- tm (token-major): SBUF [128 tok, F].
- All matmuls bf16 operands (except small fp32 K=1 broadcasts), fp32 PSUM.
"""
import numpy as np
import ml_dtypes
import concourse.bass as bass
import concourse.mybir as mybir
from contextlib import ExitStack

F32 = mybir.dt.float32
BF16 = mybir.dt.bfloat16
I32 = mybir.dt.int32
AF = mybir.ActivationFunctionType
ALU = mybir.AluOpType
AX = mybir.AxisListType

ENGINES = ("sync", "gpsimd", "tensor", "vector", "scalar")


class Rec:

    def __init__(self, ctx, name):
        self._ctx = ctx
        self._name = name

    def __getattr__(self, meth):
        ctx, name = self._ctx, self._name
        def call(*args, **kwargs):
            ctx.ops[name].append(("call", meth, args, kwargs))
            ctx.dirty[name] = True
        return call

    def inc(self, sem_name, n=1):
        ctx = self._ctx
        sem_name = ctx.semname(sem_name)
        ops = ctx.ops[self._name]
        assert any(o[0] == "call" for o in ops), "inc needs a prior instruction"
        assert ops[-1][0] == "call", "inc must directly follow its instruction"
        ops.append(("inc", sem_name, n))
        ctx.n.setdefault(sem_name, 0)
        ctx.n[sem_name] += n
        ctx.mark[self._name] = (sem_name, ctx.n[sem_name])
        ctx.dirty[self._name] = False
        return ctx.n[sem_name]

    def wait(self, sem_name, val=None):
        ctx = self._ctx
        sem_name = ctx.semname(sem_name)
        ctx.n.setdefault(sem_name, 0)
        if val is None:
            val = ctx.n[sem_name]
        ctx.ops[self._name].append(("wait", sem_name, val))

    def sem_bump(self, sem_name, n=1):
        ctx = self._ctx
        sem_name = ctx.semname(sem_name)
        ctx.ops[self._name].append(("bump", sem_name, n))
        ctx.n.setdefault(sem_name, 0)
        ctx.n[sem_name] += n
        return ctx.n[sem_name]

    def dma(self, out, in_, tag="d", **kw):
        ctx = self._ctx
        sem = ctx.semname(tag + "_" + self._name)
        ctx.n.setdefault(sem, 0)
        ctx.ops[self._name].append(("call", "dma_start", (), dict(out=out, in_=in_, **kw)))
        ctx.ops[self._name].append(("inc", sem, 16))
        ctx.n[sem] += 16
        ctx.dma_marks.setdefault(sem, 0)
        ctx.dma_marks[sem] = ctx.n[sem]
        return (sem, ctx.n[sem])

    def dmaT(self, out, in_, tag="d"):
        ctx = self._ctx
        sem = ctx.semname(tag + "_" + self._name)
        ctx.n.setdefault(sem, 0)
        ctx.ops[self._name].append(("call", "dma_start_transpose", (out, in_), {}))
        ctx.ops[self._name].append(("inc", sem, 16))
        ctx.n[sem] += 16
        ctx.dma_marks.setdefault(sem, 0)
        ctx.dma_marks[sem] = ctx.n[sem]
        return (sem, ctx.n[sem])

    def wait_dma(self, val=None):
        self.wait("d_" + self._name, val)


class SemDict(dict):

    def __init__(self, ctx):
        super().__init__()
        self._ctx = ctx

    def __getitem__(self, k):
        return super().__getitem__(self._ctx.semname(k))

    def __setitem__(self, k, v):
        super().__setitem__(self._ctx.semname(k), v)

    def __contains__(self, k):
        return super().__contains__(self._ctx.semname(k))

    def get(self, k, d=None):
        return super().get(self._ctx.semname(k), d)

    def setdefault(self, k, d=None):
        return super().setdefault(self._ctx.semname(k), d)


class Ctx:
    def __init__(self, name="prog", race=True):
        self.nc = bass.Bass("TRN2", target_bir_lowering=False, debug=False,
                            detect_race_conditions=race)
        self.stack = ExitStack()
        self.cur_stack = self.stack
        self.prefix = ""
        self.pooling = False
        self.sem_alias = {}
        self.pool_used = 0
        self.ops = {e: [] for e in ENGINES}
        self.n = SemDict(self)
        self._sems = {}
        self._consts = {}
        self._uid = 0
        for e in ENGINES:
            self.n["d_" + e] = 0
        self.dirty = {e: False for e in ENGINES}
        self.mark = {e: None for e in ENGINES}
        self.dma_marks = {}
        self.t = Rec(self, "tensor")
        self.v = Rec(self, "vector")
        self.s = Rec(self, "scalar")
        self.g = Rec(self, "gpsimd")
        self.y = Rec(self, "sync")
        self.rec = {"tensor": self.t, "vector": self.v, "scalar": self.s,
                    "gpsimd": self.g, "sync": self.y}

    def uid(self):
        self._uid += 1
        return self._uid

    def sbuf(self, name, shape, dt=BF16):
        return self.cur_stack.enter_context(
            self.nc.sbuf_tensor(self.prefix + name, list(shape), dt))

    def psum(self, name, shape, dt=F32):
        return self.cur_stack.enter_context(
            self.nc.psum_tensor(self.prefix + name, list(shape), dt))

    def semname(self, n):
        if not self.pooling or n.startswith("zz"):
            return n
        if n.endswith("_gpsimd"):
            return n
        if n.startswith("ln_"):
            return n
        if n in ("tr", "tre", "tr_e", "tr_m", "tr_ev", "tr_mm"):
            return n
        if n not in self.sem_alias:
            self.sem_alias[n] = "zz%02d" % self.pool_used
            self.pool_used += 1
        return self.sem_alias[n]

    def begin_phase(self, prefix):
        assert self.cur_stack is self.stack, "nested phase"
        self.cur_stack = ExitStack()
        self.prefix = prefix
        self.pooling = True
        self.sem_alias = {}
        self.pool_used = 0

    def end_phase(self):
        self.cur_stack.close()
        self.cur_stack = self.stack
        self.prefix = ""
        self.pooling = False

    def sem(self, name):
        if name not in self._sems:
            self._sems[name] = self.stack.enter_context(self.nc.semaphore(name))
            self.n.setdefault(name, 0)
        return self._sems[name]

    def din(self, name, shape, dt):
        return self.nc.dram_tensor(name, list(shape), dt, kind="ExternalInput").ap()

    def dout(self, name, shape, dt):
        return self.nc.dram_tensor(name, list(shape), dt, kind="ExternalOutput").ap()

    def dint(self, name, shape, dt):
        return self.nc.dram_tensor(name, list(shape), dt, kind="Internal")

    def const(self, name, arr):
        if name not in self._consts:
            self._consts[name] = self.nc.inline_tensor(np.ascontiguousarray(arr), name=name).ap()
        return self._consts[name]

    def std_consts(self):
        if getattr(self, "_cmap", None) is not None:
            return self._cmap
        c = {}
        c["ones1_f"] = self.sbuf("c_ones1f", [1, 128], F32)
        c["ones1_b"] = self.sbuf("c_ones1b", [1, 128], BF16)
        c["onesc_b"] = self.sbuf("c_onescb", [128, 1], BF16)
        c["ident_b"] = self.sbuf("c_identb", [128, 128], BF16)
        c["maskT_b"] = self.sbuf("c_masktb", [128, 128], BF16)
        c["ones11_f"] = self.sbuf("c_ones11f", [1, 1], F32)
        self.dmb = self.sbuf("c_drv", [1, 1], F32)
        self.v.memset(self.dmb[:], 0.0)
        self._cmap = c
        srcs = {
            "ones1_f": self.const("k_ones1f", np.ones((1, 128), np.float32)),
            "ones1_b": self.const("k_ones1b", np.ones((1, 128), ml_dtypes.bfloat16)),
            "onesc_b": self.const("k_onescb", np.ones((128, 1), ml_dtypes.bfloat16)),
            "ident_b": self.const("k_identb", np.eye(128, dtype=ml_dtypes.bfloat16)),
            "maskT_b": self.const("k_masktb", np.triu(np.ones((128, 128))).astype(ml_dtypes.bfloat16)),
            "ones11_f": self.const("k_ones11f", np.ones((1, 1), np.float32)),
        }
        for k, dst in c.items():
            self.y.dma(dst[:], srcs[k])
        return c

    def build(self):
        nc = self.nc
        for sname in list(self.n.keys()):
            self.sem(sname)
        with nc.Block() as block:
            def run_engine(eng_obj, name):
                last_inst = None
                for op in self.ops[name]:
                    kind = op[0]
                    if kind == "call":
                        _, meth, args, kwargs = op
                        last_inst = getattr(eng_obj, meth)(*args, **kwargs)
                    elif kind == "inc":
                        _, semn, nn = op
                        assert last_inst is not None
                        last_inst.then_inc(self._sems[semn], nn)
                    elif kind == "wait":
                        _, semn, val = op
                        eng_obj.wait_ge(self._sems[semn], val)
                    elif kind == "bump":
                        _, semn, nn = op
                        eng_obj.sem_inc(self._sems[semn], nn)
            block.sync(lambda e: run_engine(e, "sync"))
            block.gpsimd(lambda e: run_engine(e, "gpsimd"))
            block.tensor(lambda e: run_engine(e, "tensor"))
            block.vector(lambda e: run_engine(e, "vector"))
            block.scalar(lambda e: run_engine(e, "scalar"))
        return nc

    def barrier(self, engines=ENGINES):
        marks = []
        for e in engines:
            if self.dirty[e]:
                ops = self.ops[e]
                j = len(ops) - 1
                while j >= 0 and ops[j][0] != "call":
                    j -= 1
                assert j >= 0
                fs = self.semname("fl_" + e)
                self.n.setdefault(fs, 0)
                if j + 1 < len(ops) and ops[j + 1][0] == "inc":
                    self.mark[e] = (ops[j + 1][1], None)
                    raise AssertionError("dirty engine with inc'd last op")
                ops.insert(j + 1, ("inc", fs, 1))
                self.n[fs] += 1
                self.mark[e] = (fs, self.n[fs])
                self.dirty[e] = False
            if self.mark[e] is not None:
                marks.append(self.mark[e])
        for sem, cnt in list(self.dma_marks.items()):
            marks.append((sem, cnt))
        self.dma_marks.clear()
        for e in engines:
            for sem, cnt in marks:
                self.ops[e].append(("wait", sem, cnt))



def fmv(sb, T, c, lo=None, hi=None):
    if lo is None:
        lo, hi = 0, T
    return sb[:, c * T + lo: c * T + hi]


def mm_acc(C, ps, pairs, mm_sem="mm", start=True):
    t = C.t
    n = len(pairs)
    for i, (l, r) in enumerate(pairs):
        t.matmul(ps, l, r, start=(start and i == 0), stop=(i == n - 1))
    return t.inc(mm_sem)



def ln_plain(C, src, dst, T, buf, eps=1e-5):
    t, v, s, = C.t, C.v, C.s
    sc = "ln_sc"
    mq = "ln_mq"
    mm = "ln_mm"
    est = "ln_est"
    vr = "ln_vr"
    s2 = "ln_s2"
    vx = "ln_vx"
    mb = "ln_mb"
    eb = "ln_eb"
    for n in (sc, mq, mm, est, vr, s2, vx, mb, eb):
        C.n.setdefault(n, 0)
    goff = C.n[eb]
    ngroups = T // 512
    onesc = C._cmap["onesc_b"]
    ones1f = C._cmap["ones1_f"]
    for g in range(ngroups):
        gg = goff + g
        lo = g * 512
        for c in range(8):
            if C.n[sc] >= 2:
                s.wait(mq, C.n[sc] - 1)
            s.activation(out=buf["sq"][:, (c % 2) * 512:(c % 2 + 1) * 512],
                         in_=src[:, c * T + lo: c * T + lo + 512], func=AF.Square)
            s.inc(sc)
        if gg > 0:
            t.wait(est, gg)
        base_sc = C.n[sc] - 8
        for c in range(8):
            t.matmul(buf["ps_sum"], onesc[:], src[:, c * T + lo: c * T + lo + 512],
                     start=(c == 0), stop=(c == 7))
        for c in range(8):
            t.wait(sc, base_sc + c + 1)
            t.matmul(buf["ps_sq"], onesc[:], buf["sq"][:, (c % 2) * 512:(c % 2 + 1) * 512],
                     start=(c == 0), stop=(c == 7))
            t.inc(mq)
        v.wait(mq, 8 * (gg + 1))
        v.tensor_scalar(buf["rows_m"], buf["ps_sum"], 1.0 / 1024, None, ALU.mult)
        v.tensor_scalar(buf["rows_q"], buf["ps_sq"], 1.0 / 1024, None, ALU.mult)
        v.inc(est)
        v.tensor_copy(C.dmb[:], C.dmb[:])
        v.tensor_tensor(buf["rows_v"], buf["rows_m"], buf["rows_m"], ALU.mult)
        v.tensor_copy(C.dmb[:], C.dmb[:])
        v.tensor_tensor(buf["rows_v2"], buf["rows_q"], buf["rows_v"], ALU.subtract)
        v.tensor_scalar(buf["rows_v2"], buf["rows_v2"], float(eps), None, ALU.add)
        v.inc(vr)
        s.wait(vr, gg + 1)
        s.activation(out=buf["rows_sd"], in_=buf["rows_v2"], func=AF.Sqrt)
        s.inc(s2)
        v.wait(s2, gg + 1)
        recip_rows(C, buf["rows_rstd"], buf["rows_sd"], "drv")
        v.tensor_tensor(buf["rows_mrstd"], buf["rows_m"], buf["rows_rstd"], ALU.mult)
        v.inc(vx)
        t.wait(vx, gg + 1)
        if gg > 0:
            t.wait(eb, gg)
        t.matmul(buf["ps_b1"], ones1f[:], buf["rows_rstd"], start=True, stop=True)
        t.matmul(buf["ps_b2"], ones1f[:], buf["rows_mrstd"], start=True, stop=True)
        t.inc(mb)
        v.wait(mb, gg + 1)
        for c in range(8):
            v.tensor_tensor(buf["tmp"], src[:, c * T + lo: c * T + lo + 512],
                            buf["ps_b1"], ALU.mult)
            v.tensor_tensor(dst[:, c * T + lo: c * T + lo + 512], buf["tmp"],
                            buf["ps_b2"], ALU.subtract)
        v.inc(eb)
    return


def ln_bufs(C, tag=""):
    return {
        "sq": C.sbuf(f"lnsq{tag}", [128, 1024], BF16)[:],
        "tmp": C.sbuf(f"lntmp{tag}", [128, 512], BF16)[:],
        "rows_m": C.sbuf(f"lnm{tag}", [1, 512], F32)[:],
        "rows_q": C.sbuf(f"lnq{tag}", [1, 512], F32)[:],
        "rows_v": C.sbuf(f"lnv{tag}", [1, 512], F32)[:],
        "rows_v2": C.sbuf(f"lnv2{tag}", [1, 512], F32)[:],
        "rows_sd": C.sbuf(f"lnsd{tag}", [1, 512], F32)[:],
        "rows_rstd": C.sbuf(f"lnrs{tag}", [1, 512], F32)[:],
        "rows_mrstd": C.sbuf(f"lnmr{tag}", [1, 512], F32)[:],
        "ps_sum": C.psum(f"lnps{tag}", [1, 512])[:],
        "ps_sq": C.psum(f"lnpq{tag}", [1, 512])[:],
        "ps_b1": C.psum(f"lnb1{tag}", [128, 512])[:],
        "ps_b2": C.psum(f"lnb2{tag}", [128, 512])[:],
    }



def ffn0(C, xn, r, T, wg_bf, wu_bf, wd_bf, ps_rot):
    t, v, s, y = C.t, C.v, C.s, C.y
    u = C.uid()
    f_mm = f"ff_mm{u}"; f_sil = f"ff_sil{u}"
    f_stt = f"ff_stt{u}"; f_cp = f"ff_cp{u}"; f_radd = f"ff_ra{u}"
    for nm in (f_mm, f_sil, f_stt, f_cp, f_radd):
        C.n.setdefault(nm, 0)
    wT_g = C.sbuf(f"wTg{u}", [128, 8 * 768], BF16)
    wT_u = C.sbuf(f"wTu{u}", [128, 8 * 768], BF16)
    wT_d = C.sbuf(f"wTd{u}", [128, 6 * 1024], BF16)
    sb_g = C.sbuf(f"ffg{u}", [128, 6 * 512], BF16)
    sb_h = C.sbuf(f"ffh{u}", [128, 6 * 512], BF16)
    sb_t2 = C.sbuf(f"fft2{u}", [128, 2 * 512], BF16)
    nps = len(ps_rot)
    n_tile = 0
    marks = []
    ntg = T // 512
    cp_marks = []
    for hq in range(4):
        hro = hq * 768
        y.wait(f_mm, C.n[f_mm])
        for dc in range(8):
            mkk = y.dmaT(wT_g[:, dc * 768:(dc + 1) * 768],
                         wg_bf.ap()[hro:hro + 768, dc * 128:(dc + 1) * 128], tag="dw")
        for dc in range(8):
            mkk = y.dmaT(wT_u[:, dc * 768:(dc + 1) * 768],
                         wu_bf.ap()[hro:hro + 768, dc * 128:(dc + 1) * 128], tag="dw")
        for hc in range(6):
            mkk = y.dmaT(wT_d[:, hc * 1024:(hc + 1) * 1024],
                         wd_bf.ap()[:, (hq * 6 + hc) * 128:(hq * 6 + hc + 1) * 128], tag="dw")
        t.wait(*mkk)
        for tgj in range(ntg):
            lo = tgj * 512
            it = hq * ntg + tgj
            for hm in range(6):
                ps = ps_rot[n_tile % nps]
                if n_tile >= nps:
                    t.wait(*marks[n_tile - nps])
                mm_acc(C, ps[:], [(wT_g[:, dc * 768 + hm * 128: dc * 768 + (hm + 1) * 128],
                                   xn[:, dc * T + lo: dc * T + lo + 512]) for dc in range(8)], f_mm)
                n_tile += 1
                s.wait(f_mm, C.n[f_mm])
                if it > 0:
                    s.wait(f_stt, (it - 1) * 6 + hm + 1)
                s.activation(out=sb_g[:, hm * 512:(hm + 1) * 512], in_=ps[:],
                             func=AF.Silu, scale=1.0 / 127.0)
                s.inc(f_sil)
                marks.append((f_sil, C.n[f_sil]))
            dn_mm_prev = C.n[f_mm] - 12 if it > 0 else None
            for hm in range(6):
                ps = ps_rot[n_tile % nps]
                if n_tile >= nps:
                    t.wait(*marks[n_tile - nps])
                mm_acc(C, ps[:], [(wT_u[:, dc * 768 + hm * 128: dc * 768 + (hm + 1) * 128],
                                   xn[:, dc * T + lo: dc * T + lo + 512]) for dc in range(8)], f_mm)
                n_tile += 1
                v.wait(f_mm, C.n[f_mm])
                v.wait(f_sil, it * 6 + hm + 1)
                v.scalar_tensor_tensor(sb_h[:, hm * 512:(hm + 1) * 512],
                                       sb_g[:, hm * 512:(hm + 1) * 512], 1.0 / 127.0,
                                       ps[:], ALU.mult, ALU.mult)
                v.inc(f_stt)
                marks.append((f_stt, C.n[f_stt]))
            t.wait(f_stt, C.n[f_stt])
            for dm in range(8):
                ps = ps_rot[n_tile % nps]
                if n_tile >= nps:
                    t.wait(*marks[n_tile - nps])
                mm_acc(C, ps[:], [(wT_d[:, hc * 1024 + dm * 128: hc * 1024 + (dm + 1) * 128],
                                   sb_h[:, hc * 512:(hc + 1) * 512]) for hc in range(6)], f_mm)
                n_tile += 1
                s.wait(f_mm, C.n[f_mm])
                if len(cp_marks) >= 2:
                    s.wait(f_radd, cp_marks[-2])
                s.activation(out=sb_t2[:, (dm % 2) * 512:(dm % 2 + 1) * 512], in_=ps[:],
                             func=AF.Copy, scale=1.0 / 127.0)
                s.inc(f_cp)
                marks.append((f_cp, C.n[f_cp]))
                v.wait(f_cp, C.n[f_cp])
                v.tensor_tensor(r[:, dm * T + lo: dm * T + lo + 512],
                                r[:, dm * T + lo: dm * T + lo + 512],
                                sb_t2[:, (dm % 2) * 512:(dm % 2 + 1) * 512], ALU.add)
                v.inc(f_radd)
                cp_marks.append(C.n[f_radd])
    C.barrier()



def build_a1(C=None, io=None):
    if C is None:
        C = Ctx("a1")
    T = 4096
    if io is None:
        io = {"x0": C.din("x0", [1024, T], F32),
              "wg": C.din("wg", [3072, 1024], I32),
              "wu": C.din("wu", [3072, 1024], I32),
              "wd": C.din("wd", [1024, 3072], I32),
              "xn_o": C.dout("xn_o", [1024, T], BF16),
              "r_o": C.dout("r_o", [1024, T], BF16)}
    x0 = io["x0"]; wg_d = io["wg"]; wu_d = io["wu"]; wd_d = io["wd"]
    xn_o = io["xn_o"]; r_o = io["r_o"]
    wg_bf = C.dint("wg_bf", [3072, 1024], BF16)
    wu_bf = C.dint("wu_bf", [3072, 1024], BF16)
    wd_bf = C.dint("wd_bf", [1024, 3072], BF16)

    r = C.sbuf("r", [128, 8 * T], BF16)
    xn = C.sbuf("xn", [128, 8 * T], BF16)
    ps_rot = [C.psum(f"ps{i}", [128, 512]) for i in range(4)]
    C.std_consts()
    lb = ln_bufs(C)

    C.g.dma(r[:].rearrange("p (c t) -> p c t", c=8),
            x0.rearrange("(c p) t -> p c t", p=128))
    C.g.dma(wg_bf.ap(), wg_d)
    C.g.dma(wu_bf.ap(), wu_d)
    C.g.dma(wd_bf.ap(), wd_d)
    C.barrier()

    ln_plain(C, r[:], xn[:], T, lb)
    C.barrier()

    ffn0(C, xn[:], r[:], T, wg_bf, wu_bf, wd_bf, ps_rot)

    C.y.dma(xn_o.rearrange("(c p) t -> p c t", p=128),
            xn[:].rearrange("p (c t) -> p c t", c=8))
    C.y.dma(r_o.rearrange("(c p) t -> p c t", p=128),
            r[:].rearrange("p (c t) -> p c t", c=8))
    C.y.wait_dma()
    return C



def vecs_to_cols(C, pairs, ps_t8):
    u = C.uid()
    mark = None
    rows = []
    for i, (src_ap, _) in enumerate(pairs):
        row = C.sbuf(f"v2crow{u}_{i}", [1, 1024], F32)
        rows.append(row)
        mark = C.y.dma(row[:], src_ap.rearrange("(a p) -> a p", a=1), tag=f"v2c{u}")
    C.t.wait(*mark)
    out_marks = []
    for i, (_, dst_cols) in enumerate(pairs):
        sem = f"v2cs{u}_{i}"
        row_to_cols(C, rows[i][0:1, :], 8, dst_cols, ps_t8, sem)
        out_marks.append((sem, C.n[sem]))
    return out_marks


def row_to_cols(C, row_ap, n, dst_cols, ps_t8, sem):
    C.n.setdefault(sem, 0)
    ones11 = C._cmap["ones11_f"]
    for c in range(n):
        C.t.matmul(ps_t8[:, c:c + 1], row_ap[0:1, c * 128:(c + 1) * 128], ones11[:],
                   start=True, stop=True)
    C.t.inc(sem)
    C.v.wait(sem, C.n[sem])
    C.v.tensor_copy(dst_cols[:], ps_t8[:, 0:n])
    C.v.inc(sem)
    C.t.wait(sem, C.n[sem])



def build_a2(C=None, io=None):
    if C is None:
        C = Ctx("a2")
    T = 4096
    if io is None:
        io = {"xn": C.din("xn", [1024, T], BF16),
              "Wq": C.din("Wq", [1024, 1024], F32),
              "Wk": C.din("Wk", [1024, 1024], F32),
              "Wv": C.din("Wv", [1024, 1024], F32),
              "g1": C.din("g1", [1024], F32),
              "b1": C.din("b1", [1024], F32),
              "qf_o": C.dout("qf_o", [1024, T], BF16),
              "kf_o": C.dout("kf_o", [1024, T], BF16),
              "kt_o": C.dout("kt_o", [T, 1024], BF16),
              "vt_o": C.dout("vt_o", [T, 1024], BF16),
              "st_o": C.dout("st_o", [64, 16 * 65], F32)}
    xn_d = io["xn"]; wq_d = io["Wq"]; wk_d = io["Wk"]; wv_d = io["Wv"]
    g1_d = io["g1"]; b1_d = io["b1"]
    qf_o = io["qf_o"]; kf_o = io["kf_o"]; kt_o = io["kt_o"]
    vt_o = io["vt_o"]; st_o = io["st_o"]

    xn = C.sbuf("s_xn", [128, 8 * T], BF16)
    w_q = C.sbuf("s_wq", [128, 8 * 1024], BF16)
    w_k = C.sbuf("s_wk", [128, 8 * 1024], BF16)
    w_v = C.sbuf("s_wv", [128, 8 * 1024], BF16)
    stg = C.sbuf("s_stg", [128, T], BF16)
    ktT = C.sbuf("s_ktT", [128, 32 * 128], BF16)
    vtT = C.sbuf("s_vtT", [128, 32 * 130], BF16)
    ph_t = C.sbuf("s_pht", [128, 2 * 512], BF16)
    ph_u = C.sbuf("s_phu", [128, 512], BF16)
    ph_e = C.sbuf("s_phe", [128, 512], BF16)
    st_sb = C.sbuf("s_stsb", [64, 16 * 65], F32)
    a1c = C.sbuf("s_a1c", [128, 8], F32)
    b1c = C.sbuf("s_b1c", [128, 8], F32)
    b1cb = C.sbuf("s_b1cb", [128, 8], BF16)
    rb_row = C.sbuf("s_rbrow", [1, 3 * 1024], F32)
    rbc = C.sbuf("s_rbc", [128, 24], F32)
    ps_a = C.psum("ps_a", [128, 512])
    ps_b = C.psum("ps_b", [128, 512])
    ps_t = C.psum("ps_t", [128, 128], BF16)
    ps_v8 = C.psum("ps_v8", [128, 8])
    ps_st = [C.psum(f"ps_st{i}", [64, 260]) for i in range(4)]
    cm = C.std_consts()
    cm["ident_f"] = C.sbuf("c_identf", [128, 128], F32)
    C.y.dma(cm["ident_f"][:], C.const("k_identf", np.eye(128, dtype=np.float32)))

    t, v, s, y, g = C.t, C.v, C.s, C.y, C.g
    y.dma(xn[:].rearrange("p (c t) -> p c t", c=8), xn_d.rearrange("(c p) t -> p c t", p=128))
    g.dma(w_q[:].rearrange("p (k n) -> p k n", k=8), wq_d.rearrange("(k p) n -> p k n", p=128))
    g.dma(w_k[:].rearrange("p (k n) -> p k n", k=8), wk_d.rearrange("(k p) n -> p k n", p=128))
    g.dma(w_v[:].rearrange("p (k n) -> p k n", k=8), wv_d.rearrange("(k p) n -> p k n", p=128))
    C.barrier()
    for _m in vecs_to_cols(C, [(g1_d, a1c[:]), (b1_d, b1c[:])], ps_v8[:]):
        v.wait(*_m)
    v.tensor_scalar(a1c[:], a1c[:], float(1.0 / np.sqrt(1.0 + 1e-5)), None, ALU.mult)
    v.tensor_copy(b1cb[:], b1c[:])
    v.inc("fold_v")
    C.barrier()
    for wi, w_sb in enumerate((w_q, w_k, w_v)):
        for half in range(2):
            ps = (ps_a, ps_b)[half]
            mm_acc(C, ps[0:1, :],
                   [(b1cb[:, k:k + 1], w_sb[:, k * 1024 + half * 512: k * 1024 + half * 512 + 512])
                    for k in range(8)], "rbm")
        v.wait("rbm", C.n["rbm"])
        v.tensor_copy(rb_row[0:1, wi * 1024: wi * 1024 + 512], ps_a[0:1, :])
        v.tensor_copy(rb_row[0:1, wi * 1024 + 512: wi * 1024 + 1024], ps_b[0:1, :])
        v.inc("rbv")
        t.wait("rbv", C.n["rbv"])
    for wi in range(3):
        row_to_cols(C, rb_row[0:1, wi * 1024:(wi + 1) * 1024], 8,
                    rbc[:, wi * 8:(wi + 1) * 8], ps_v8[:], f"r2c{wi}")
    for w_sb in (w_q, w_k, w_v):
        for k in range(8):
            v.tensor_scalar(w_sb[:, k * 1024:(k + 1) * 1024],
                            w_sb[:, k * 1024:(k + 1) * 1024], a1c[:, k:k + 1], None, ALU.mult)
    v.inc("fold_v")
    C.barrier()
    v.memset(vtT[:].rearrange("p (a c b) -> p a c b", a=32, c=2)[:, :, :, 64:65], 1.0)
    C.barrier()

    for nm in ("q_mm", "ph_s", "ph_v", "tr_m", "tr_e", "st_m", "st_ev", "rbm", "rbv"):
        C.n.setdefault(nm, 0)
    n_tile = 0
    marks = []
    bank_started = [False] * 4
    stg_guard = []

    def phi_evac(ps, mcol, dst_slice, tg):
        s.wait("q_mm", C.n["q_mm"])
        s.activation(out=ph_t[:, (tg % 2) * 512:(tg % 2 + 1) * 512], in_=ps[:],
                     func=AF.Relu, bias=rbc[:, mcol:mcol + 1])
        s.inc("ph_s")
        v.wait("ph_s", C.n["ph_s"])
        v.scalar_tensor_tensor(ph_u[:], ps[:], rbc[:, mcol:mcol + 1],
                               ph_t[:, (tg % 2) * 512:(tg % 2 + 1) * 512], ALU.add, ALU.subtract)
        v.inc("ph_v")
        s.wait("ph_v", C.n["ph_v"])
        s.activation(out=ph_e[:], in_=ph_u[:], func=AF.Exp)
        s.inc("ph_s")
        v.wait("ph_s", C.n["ph_s"])
        v.tensor_tensor(dst_slice, ph_e[:], ph_t[:, (tg % 2) * 512:(tg % 2 + 1) * 512], ALU.add)
        v.inc("ph_v")
        return ("ph_v", C.n["ph_v"])

    for m in range(8):
        for proj, w_sb in (("q", w_q), ("k", w_k), ("v", w_v)):
            sg = list(stg_guard)
            stg_guard = []
            for tg in range(8):
                ps = (ps_a, ps_b)[n_tile % 2]
                if n_tile >= 2:
                    t.wait(*marks[n_tile - 2])
                mm_acc(C, ps[:], [(w_sb[:, k * 1024 + m * 128:k * 1024 + (m + 1) * 128],
                                   xn[:, k * T + tg * 512: k * T + (tg + 1) * 512])
                                  for k in range(8)], "q_mm")
                n_tile += 1
                dst = stg[:, tg * 512:(tg + 1) * 512]
                if tg == 0:
                    for gd in sg:
                        s.wait(*gd)
                        v.wait(*gd)
                if proj in ("q", "k"):
                    marks.append(phi_evac(ps, {"q": 0, "k": 8, "v": 16}[proj] + m, dst, tg))
                else:
                    v.wait("q_mm", C.n["q_mm"])
                    v.tensor_scalar(dst, ps[:], rbc[:, 16 + m:16 + m + 1], None, ALU.add)
                    v.inc("ph_v")
                    marks.append(("ph_v", C.n["ph_v"]))
            ev_sem, ev_val = ("ph_v", C.n["ph_v"])
            stag = "s%d" % (m % 2)
            if proj == "q":
                y.wait(ev_sem, ev_val)
                stg_guard.append(y.dma(qf_o.rearrange("(c p) t -> p c t", p=128)[:, m:m + 1, :],
                                 stg[:].rearrange("p (a t) -> p a t", a=1), tag=stag))
            else:
                if proj == "k":
                    y.wait(ev_sem, ev_val)
                    stg_guard.append(y.dma(kf_o.rearrange("(c p) t -> p c t", p=128)[:, m:m + 1, :],
                                     stg[:].rearrange("p (a t) -> p a t", a=1), tag=stag))
                t.wait(ev_sem, ev_val)
                if m > 0:
                    v.wait("st_m", C.n["st_m"])
                for tc in range(32):
                    t.wait("tr_e", C.n["tr_m"])
                    t.transpose(ps_t[:], stg[:, tc * 128:(tc + 1) * 128], cm["ident_b"][:])
                    t.inc("tr_m")
                    v.wait("tr_m", C.n["tr_m"])
                    if proj == "k":
                        v.tensor_copy(ktT[:, tc * 128:(tc + 1) * 128], ps_t[:])
                    else:
                        v.tensor_copy(
                            vtT[:].rearrange("p (a c b) -> p a c b", a=32, c=2)[:, tc, :, 0:64],
                            ps_t[:].rearrange("p (c b) -> p c b", c=2))
                    v.inc("tr_e")
                stg_guard.append(("tr_e", C.n["tr_e"]))
        t.wait("tr_e", C.n["tr_e"])
        for tc in range(32):
            for hl in range(2):
                h = 2 * m + hl
                bank = h // 4
                reg = h % 4
                t.matmul(ps_st[bank][:, reg * 65:(reg + 1) * 65],
                         ktT[:, tc * 128 + hl * 64: tc * 128 + hl * 64 + 64],
                         vtT[:, tc * 130 + hl * 65: tc * 130 + (hl + 1) * 65],
                         start=(not bank_started[bank]), stop=(tc == 31 and hl == 1 and m % 2 == 1))
                bank_started[bank] = True
        t.inc("st_m")
        y.wait("st_m", C.n["st_m"])
        stg_guard.append(y.dma(kt_o[:, m * 128:(m + 1) * 128].rearrange("(a p) b -> p a b", p=128),
              ktT[:].rearrange("p (a b) -> p a b", a=32), tag=stag))
        for hl in range(2):
            stg_guard.append(y.dma(vt_o[:, m * 128 + hl * 64: m * 128 + (hl + 1) * 64]
                  .rearrange("(a p) b -> p a b", p=128),
                  vtT[:].rearrange("p (a q) -> p a q", a=32)[:, :, hl * 65: hl * 65 + 64], tag=stag))
    v.wait("st_m", C.n["st_m"])
    for bank in range(4):
        v.tensor_copy(st_sb[:, bank * 260:(bank + 1) * 260], ps_st[bank][:])
    v.inc("st_ev")
    y.wait("st_ev", C.n["st_ev"])
    y.dma(st_o, st_sb[:])
    C.barrier()
    return C



def build_b(debug=False, race=True, C=None, io=None):
    if C is None:
        C = Ctx("b", race=race)
    T = 4096
    TH = 2048
    if io is None:
        io = {"xn": C.din("xn", [1024, T], BF16),
              "r1p": C.din("r1p", [1024, T], BF16),
              "qf": C.din("qf", [1024, T], BF16),
              "kf": C.din("kf", [1024, T], BF16),
              "kt": C.din("kt", [T, 1024], BF16),
              "vt": C.din("vt", [T, 1024], BF16),
              "carry": C.din("carry", [64, 16 * 65], F32),
              "Wo": C.din("Wo", [1024, 1024], F32),
              "g2": C.din("g2", [1024], F32),
              "b2": C.din("b2", [1024], F32),
              "Wf1": C.din("Wf1", [1024, 4096], F32),
              "Wf2": C.din("Wf2", [4096, 1024], F32),
              "x1_o": C.dout("x1_o", [1024, T], BF16)}
    xn_d = io["xn"]; r1p_d = io["r1p"]; qf_d = io["qf"]; kf_d = io["kf"]
    kt_d = io["kt"]; vt_d = io["vt"]; carry_d = io["carry"]; wo_d = io["Wo"]
    g2_d = io["g2"]; b2_d = io["b2"]; wf1_d = io["Wf1"]; wf2_d = io["Wf2"]
    x1_o = io["x1_o"]

    r = C.sbuf("s_r", [128, 8 * TH], BF16)
    xa = C.sbuf("s_xa", [128, 8 * TH], BF16)
    afm = C.sbuf("s_afm", [128, 8 * TH], BF16)
    arena = C.sbuf("s_arena", [128, 16512], BF16)
    qh = arena[:, 0:4096]
    kh = arena[:, 4096:8192]
    kth = arena[:, 8192:12288]
    vth = arena[:, 12288:16448]
    wbig2 = arena[:, 0:8192]
    gbuf = arena[:, 8192:12288]
    lnscr = arena[:, 12288:14336]
    wbig = C.sbuf("s_wbig", [128, 8 * 1024], BF16)
    ssb = C.sbuf("s_ssb", [128, 260], BF16)
    ssf = C.sbuf("s_ssf", [64, 260], F32)
    ssave = C.sbuf("s_ssave", [64, 16 * 65], F32)
    carry = C.sbuf("s_carry", [64, 16 * 65], F32)
    shat = C.sbuf("s_shat", [128, 128], BF16)
    opair = C.sbuf("s_opair", [128, 128], BF16)
    dmx = C.sbuf("s_dmx", [128, 4], F32)
    rden = C.sbuf("s_rden", [128, 4], F32)
    i64f = C.sbuf("s_i64f", [64, 64], F32)
    a2c = C.sbuf("s_a2c", [128, 8], F32)
    b2c = C.sbuf("s_b2c", [128, 8], F32)
    b2cb = C.sbuf("s_b2cb", [128, 8], BF16)
    rb2_row = C.sbuf("s_rb2row", [1, 1024], F32)
    rb2c = C.sbuf("s_rb2c", [128, 32], F32)
    ln_rows = {nm: C.sbuf(f"s_lnr_{nm}", [1, 512], F32)[:] for nm in
               ("m", "q", "v", "v2", "sd", "rstd", "mrstd")}
    P_sc = C.psum("P_sc", [128, 128])
    P_st = C.psum("P_st", [64, 260])
    P_O = C.psum("P_O", [128, 260])
    P_tr = C.psum("P_tr", [128, 128], BF16)
    P_a = C.psum("P_a", [128, 512])
    P_b = C.psum("P_b", [128, 512])
    P_r1 = C.psum("P_r1", [1, 512])
    P_r2 = C.psum("P_r2", [1, 512])
    cm = C.std_consts()
    t, v, s, y, g = C.t, C.v, C.s, C.y, C.g
    if debug:
        dbg_afm = C.dout("dbg_afm", [2, 1024, 2048], BF16)
        dbg_xa = C.dout("dbg_xa", [2, 1024, 2048], BF16)

    mcr = g.dma(carry[:], carry_d)
    g.dma(i64f[:], C.const("k_i64f", np.eye(64, dtype=np.float32)))
    g.dma(wbig[:].rearrange("p (k n) -> p k n", k=8), wo_d.rearrange("(k p) n -> p k n", p=128))
    if io.get("ism") is not None:
        ismc = C.sbuf("s_ismc", [64, 1], F32)
        mis = g.dma(ismc[:], io["ism"])
        v.wait(*mcr)
        v.wait(*mis)
        v.tensor_scalar(carry[:], carry[:], ismc[:, 0:1], None, ALU.mult)
        v.inc("crm")
    C.barrier()
    for _m in vecs_to_cols(C, [(g2_d, a2c[:]), (b2_d, b2c[:])], P_a[:, 0:8]):
        v.wait(*_m)
    v.tensor_scalar(a2c[:], a2c[:], float(1.0 / np.sqrt(1.0 + 1e-5)), None, ALU.mult)
    v.tensor_copy(b2cb[:], b2c[:])
    v.inc("fold2")
    C.barrier()

    for nm in ("o_mm", "st_mm", "sf_mm", "sf2_mm", "sc_mm", "sh_ev", "st_ev",
               "o_ev", "on_ev", "tr_mm", "tr_ev", "sv_ev", "vo_ev", "wo_mm",
               "wo_ev", "ra_ev", "ge_ev", "sf2_ev", "rb2_mm", "rb2_ev", "w1f_ev", "fold2"):
        C.n.setdefault(nm, 0)
    marks = []
    n_tile = 0

    for half in range(2):
        hlo = half * TH
        g.dma(r[:].rearrange("p (c t) -> p c t", c=8),
              r1p_d.rearrange("(c p) t -> p c t", p=128)[:, :, hlo:hlo + TH], tag="dh")
        ldh = g.dma(xa[:].rearrange("p (c t) -> p c t", c=8),
              xn_d.rearrange("(c p) t -> p c t", p=128)[:, :, hlo:hlo + TH], tag="dh")
        for hg in range(4):
            g.wait("o_mm", C.n["o_mm"])
            g.wait("st_mm", C.n["st_mm"])
            g.wait("sf2_mm", C.n["sf2_mm"])
            g.dma(qh.rearrange("p (a t) -> p a t", a=2),
                  qf_d.rearrange("(c p) t -> p c t", p=128)[:, 2 * hg:2 * hg + 2, hlo:hlo + TH], tag="dg")
            g.dma(kh.rearrange("p (a t) -> p a t", a=2),
                  kf_d.rearrange("(c p) t -> p c t", p=128)[:, 2 * hg:2 * hg + 2, hlo:hlo + TH], tag="dg")
            g.dma(kth.rearrange("p (a f) -> p a f", a=16),
                  kt_d.rearrange("(u a p) f -> u p a f", u=2, p=128)[half][:, :, hg * 256:(hg + 1) * 256], tag="dg")
            for j in range(4):
                gl_mark = g.dma(vth.rearrange("p (a q) -> p a q", a=16)[:, :, j * 65: j * 65 + 64],
                      vt_d.rearrange("(u a p) f -> u p a f", u=2, p=128)[half]
                      [:, :, hg * 256 + j * 64: hg * 256 + (j + 1) * 64], tag="dg")
            v.wait(*gl_mark)
            v.memset(vth.rearrange("p (a c b) -> p a c b", a=16, c=4)[:, :, :, 64:65], 1.0)
            v.inc("vo_ev")
            t.wait(*gl_mark)
            t.wait("vo_ev", C.n["vo_ev"])
            if half == 0:
                v.tensor_copy(ssf[:], carry[:, hg * 260:(hg + 1) * 260])
            else:
                v.wait("sv_ev", C.n["sv_ev"])
                v.tensor_copy(ssf[:], ssave[:, hg * 260:(hg + 1) * 260])
            v.inc("sa_ev")
            for c in range(16):
                v.wait("o_mm", C.n["o_mm"])
                v.wait("sa_ev", C.n["sa_ev"])
                v.tensor_copy(ssb[0:64, :], ssf[:])
                v.tensor_copy(ssb[64:128, :], ssf[:])
                v.inc("st_ev")
                t.wait("sa_ev", C.n["sa_ev"])
                for j in range(4):
                    t.matmul(P_st[:, j * 65:(j + 1) * 65],
                             kth[:, c * 256 + j * 64: c * 256 + (j + 1) * 64],
                             vth[:, c * 260 + j * 65: c * 260 + (j + 1) * 65],
                             start=(j == 0), stop=(j == 3))
                t.inc("st_mm")
                v.wait("st_mm", C.n["st_mm"])
                v.tensor_tensor(ssf[:], ssf[:], P_st[:], ALU.add)
                v.inc("sa_ev")
                t.wait("st_ev", C.n["st_ev"])
                for j in range(4):
                    mch = j // 2
                    rb_ = (j % 2) * 64
                    t.wait("sh_ev", C.n["sh_ev"])
                    t.matmul(P_sc[:], kh[rb_:rb_ + 64, mch * TH + c * 128: mch * TH + (c + 1) * 128],
                             qh[rb_:rb_ + 64, mch * TH + c * 128: mch * TH + (c + 1) * 128],
                             start=True, stop=True)
                    t.inc("sc_mm")
                    v.wait("sc_mm", C.n["sc_mm"])
                    v.tensor_tensor(shat[:], P_sc[:], cm["maskT_b"][:], ALU.mult)
                    v.inc("sh_ev")
                    t.wait("sh_ev", C.n["sh_ev"])
                    if j == 0:
                        t.wait("o_ev", C.n["o_ev"])
                    t.matmul(P_O[:, j * 65:(j + 1) * 65], shat[:],
                             vth[:, c * 260 + j * 65: c * 260 + (j + 1) * 65],
                             start=(j == 0), stop=False)
                    t.matmul(P_O[:, j * 65:(j + 1) * 65],
                             qh[rb_:rb_ + 64, mch * TH + c * 128: mch * TH + (c + 1) * 128],
                             ssb[rb_:rb_ + 64, j * 65:(j + 1) * 65],
                             start=False, stop=(j == 3))
                    t.inc("o_mm")
                v.wait("o_mm", C.n["o_mm"])
                v.tensor_scalar(dmx[:].rearrange("p (a b) -> p a b", b=1),
                                P_O[:].rearrange("p (a q) -> p a q", a=4)[:, :, 64:65],
                                1e-6, None, ALU.max)
                v.inc("dx_ev")
                v.wait("dx_ev", C.n["dx_ev"])
                v.reciprocal(rden[:], dmx[:])
                v.inc("rr_ev")
                v.wait("rr_ev", C.n["rr_ev"])
                for pj in range(2):
                    for j2 in range(2):
                        j = pj * 2 + j2
                        v.tensor_scalar(opair[:, j2 * 64:(j2 + 1) * 64],
                                        P_O[:, j * 65: j * 65 + 64],
                                        rden[:, j:j + 1], None, ALU.mult)
                    v.inc("on_ev")
                    t.wait("on_ev", C.n["on_ev"])
                    t.wait("tr_ev", C.n["tr_mm"])
                    t.transpose(P_tr[:], opair[:], cm["ident_b"][:])
                    t.inc("tr_mm")
                    v.wait("tr_mm", C.n["tr_mm"])
                    v.tensor_copy(afm[:, (hg * 2 + pj) * TH + c * 128: (hg * 2 + pj) * TH + (c + 1) * 128],
                                  P_tr[:])
                    v.inc("tr_ev")
                v.tensor_copy(rden[:], rden[:])
                v.inc("o_ev")
            if half == 0:
                v.tensor_copy(ssave[:, hg * 260:(hg + 1) * 260], ssf[:])
                v.inc("sv_ev")
        if debug:
            y.wait("tr_ev", C.n["tr_ev"])
            y.dma(dbg_afm[half].rearrange("(c p) t -> p c t", p=128),
                  afm[:].rearrange("p (c t) -> p c t", c=8), tag="dbg")
        t.wait("tr_ev", C.n["tr_ev"])
        t.wait(*ldh)
        v.wait(*ldh)
        for dm in range(8):
            for tg in range(4):
                ps = (P_a, P_b)[n_tile % 2]
                if n_tile >= 2:
                    t.wait(*marks[n_tile - 2])
                mm_acc(C, ps[:], [(wbig[:, k * 1024 + dm * 128: k * 1024 + (dm + 1) * 128],
                                   afm[:, k * TH + tg * 512: k * TH + tg * 512 + 512])
                                  for k in range(8)], "wo_mm")
                n_tile += 1
                v.wait("wo_mm", C.n["wo_mm"])
                v.tensor_tensor(xa[:, dm * TH + tg * 512: dm * TH + tg * 512 + 512],
                                xa[:, dm * TH + tg * 512: dm * TH + tg * 512 + 512],
                                ps[:], ALU.add)
                v.inc("wo_ev")
                marks.append(("wo_ev", C.n["wo_ev"]))
        if debug:
            y.wait("wo_ev", C.n["wo_ev"])
            y.dma(dbg_xa[half].rearrange("(c p) t -> p c t", p=128),
                  xa[:].rearrange("p (c t) -> p c t", c=8), tag="dbg")
        for dm in range(8):
            for tg in range(4):
                v.tensor_tensor(r[:, dm * TH + tg * 512: dm * TH + tg * 512 + 512],
                                r[:, dm * TH + tg * 512: dm * TH + tg * 512 + 512],
                                xa[:, dm * TH + tg * 512: dm * TH + tg * 512 + 512], ALU.add)
        v.inc("ra_ev")
        lbuf = {"sq": lnscr[:, 0:1024], "tmp": lnscr[:, 1024:1536],
                "rows_m": ln_rows["m"], "rows_q": ln_rows["q"], "rows_v": ln_rows["v"],
                "rows_v2": ln_rows["v2"], "rows_sd": ln_rows["sd"],
                "rows_rstd": ln_rows["rstd"], "rows_mrstd": ln_rows["mrstd"],
                "ps_sum": P_r1[:], "ps_sq": P_r2[:], "ps_b1": P_a[:], "ps_b2": P_b[:]}
        C.barrier()
        ln_plain(C, xa[:], afm[:], TH, lbuf)
        C.barrier()
        xh = afm
        for hq in range(4):
            qlo = hq * 1024
            g.wait("sf_mm", C.n["sf_mm"])
            g.wait("sf2_mm", C.n["sf2_mm"])
            if hq == 0:
                g.wait("wo_mm", C.n["wo_mm"])
                g.wait("o_mm", C.n["o_mm"])
                g.wait("st_mm", C.n["st_mm"])
            g.dma(wbig[:].rearrange("p (k n) -> p k n", k=8),
                  wf1_d.rearrange("(k p) n -> p k n", p=128)[:, :, qlo:qlo + 1024], tag="dw")
            qm = g.dma(wbig2.rearrange("p (k n) -> p k n", k=8),
                  wf2_d.rearrange("(u k p) n -> u p k n", u=4, p=128)[hq], tag="dw")
            t.wait(*qm)
            if half == 0:
                for sub in range(2):
                    mm_acc(C, (P_a, P_b)[sub][0:1, :],
                           [(b2cb[:, k:k + 1], wbig[:, k * 1024 + sub * 512: k * 1024 + sub * 512 + 512])
                            for k in range(8)], "rb2_mm")
                v.wait("rb2_mm", C.n["rb2_mm"])
                v.tensor_copy(rb2_row[0:1, 0:512], P_a[0:1, :])
                v.tensor_copy(rb2_row[0:1, 512:1024], P_b[0:1, :])
                v.inc("rb2_ev")
                t.wait("rb2_ev", C.n["rb2_ev"])
                row_to_cols(C, rb2_row[0:1, 0:1024], 8, rb2c[:, hq * 8:(hq + 1) * 8],
                            P_a[:, 0:8], f"r2cq{half}_{hq}")
            v.wait(*qm)
            for k in range(8):
                v.tensor_scalar(wbig[:, k * 1024:(k + 1) * 1024],
                                wbig[:, k * 1024:(k + 1) * 1024], a2c[:, k:k + 1], None, ALU.mult)
            v.inc("w1f_ev")
            t.wait("w1f_ev", C.n["w1f_ev"])
            for tg in range(4):
                for hm in range(8):
                    ps = (P_a, P_b)[n_tile % 2]
                    if n_tile >= 2:
                        t.wait(*marks[n_tile - 2])
                    mm_acc(C, ps[:], [(wbig[:, k * 1024 + hm * 128: k * 1024 + (hm + 1) * 128],
                                       xh[:, k * TH + tg * 512: k * TH + tg * 512 + 512])
                                      for k in range(8)], "sf_mm")
                    n_tile += 1
                    s.wait("sf_mm", C.n["sf_mm"])
                    if C.n["sf2_mm"] > 0:
                        s.wait("sf2_mm", C.n["sf2_mm"])
                    s.activation(out=gbuf[:, hm * 512:(hm + 1) * 512], in_=ps[:],
                                 func=AF.Gelu, bias=rb2c[:, hq * 8 + hm: hq * 8 + hm + 1])
                    s.inc("ge_ev")
                    marks.append(("ge_ev", C.n["ge_ev"]))
                t.wait("ge_ev", C.n["ge_ev"])
                for dm in range(8):
                    ps = (P_a, P_b)[n_tile % 2]
                    if n_tile >= 2:
                        t.wait(*marks[n_tile - 2])
                    mm_acc(C, ps[:], [(wbig2[:, k * 1024 + dm * 128: k * 1024 + (dm + 1) * 128],
                                       gbuf[:, k * 512:(k + 1) * 512]) for k in range(8)], "sf2_mm")
                    n_tile += 1
                    v.wait("sf2_mm", C.n["sf2_mm"])
                    v.tensor_tensor(r[:, dm * TH + tg * 512: dm * TH + tg * 512 + 512],
                                    r[:, dm * TH + tg * 512: dm * TH + tg * 512 + 512],
                                    ps[:], ALU.add)
                    v.inc("sf2_ev")
                    marks.append(("sf2_ev", C.n["sf2_ev"]))
        C.barrier()
        y.dma(x1_o.rearrange("(c p) t -> p c t", p=128)[:, :, hlo:hlo + TH],
              r[:].rearrange("p (c t) -> p c t", c=8))
        if half == 0:
            g.dma(wbig[:].rearrange("p (k n) -> p k n", k=8),
                  wo_d.rearrange("(k p) n -> p k n", p=128), tag="dw")
        C.barrier()
    y.wait_dma()
    return C



def drain(C, sem="drv"):
    v = C.v
    C.n.setdefault(sem, 0)
    v.tensor_copy(C.dmb[:], C.dmb[:])
    v.inc(sem)
    v.wait(sem, C.n[sem])


def recip_rows(C, dst, src, n_edge):
    v = C.v
    C.n.setdefault(n_edge, 0)
    v.tensor_copy(C.dmb[:], C.dmb[:])
    v.inc(n_edge)
    v.wait(n_edge, C.n[n_edge])
    v.reciprocal(dst, src)
    v.inc(n_edge)
    v.wait(n_edge, C.n[n_edge])
    v.tensor_copy(C.dmb[:], C.dmb[:])


def build_c(race=True, debug=False, C=None, io=None):
    if C is None:
        C = Ctx("c", race=race)
    T = 4096
    TG = 512
    if io is None:
        io = {"x1": C.din("x1", [1024, T], BF16),
              "W_K": C.din("W_K", [1024, 1024], F32),
              "W_V": C.din("W_V", [1024, 1024], F32),
              "W_Q": C.din("W_Q", [1024, 1024], F32),
              "mW0": C.din("mW0", [64, 256], F32),
              "mW1": C.din("mW1", [256, 64], F32),
              "knw": C.din("knw", [64], F32),
              "qnw": C.din("qnw", [64], F32),
              "snw": C.din("snw", [1024], F32),
              "rnw": C.din("rnw", [1024], F32),
              "aw": C.din("aw", [1024, 16], F32),
              "tw": C.din("tw", [1024, 16], F32),
              "ew": C.din("ew", [1024, 16], F32),
              "ab": C.din("ab", [16], F32),
              "tb": C.din("tb", [16], F32),
              "eb": C.din("eb", [16], F32),
              "slot_init": C.din("slot_init", [64, 1024], F32),
              "w1": C.din("w1", [2048, 512], F32),
              "b1": C.din("b1", [512], F32),
              "w2": C.din("w2", [512, 64], F32),
              "b2": C.din("b2", [64], F32),
              "tau": C.din("tau", [1], F32),
              "q_o": C.dout("q_o", [1024, T], BF16),
              "alpha_o": C.dout("alpha_o", [T, 64], BF16),
              "g0_o": C.dout("g0_o", [64, 4096], F32),
              "g1_o": C.dout("g1_o", [128, 2048], F32),
              "slot_o": C.dout("slot_o", [64, 1025], F32),
              "ae_o": C.dout("ae_o", [48, 1], F32)}
    x1_d = io["x1"]; wk_d = io["W_K"]; wv_d = io["W_V"]; wq_d = io["W_Q"]
    mw0_d = io["mW0"]; mw1_d = io["mW1"]; knw_d = io["knw"]; qnw_d = io["qnw"]
    snw_d = io["snw"]; rnw_d = io["rnw"]; aw_d = io["aw"]; tw_d = io["tw"]
    ew_d = io["ew"]; ab_d = io["ab"]; tb_d = io["tb"]; eb_d = io["eb"]
    sli_d = io["slot_init"]; w1_d = io["w1"]; b1_d = io["b1"]; w2_d = io["w2"]
    b2_d = io["b2"]; tau_d = io["tau"]
    q_o = io["q_o"]; alpha_o = io["alpha_o"]; g0_o = io["g0_o"]
    g1_o = io["g1_o"]; slot_o = io["slot_o"]; ae_o = io["ae_o"]
    if debug:
        dbgH_o = C.dout("dbgH_o", [128, 128], F32)
        dha_o = C.dout("dha_o", [128, 4096], BF16)
        dbgJ_o = C.dout("dbgJ_o", [128, 160], F32)
        hbuf_o = C.dout("hbuf_o", [128, 2048], BF16)
        pre1a_o = C.dout("pre1a_o", [128, 4096], BF16)
        ktm_o = C.dout("ktm_o", [128, 4096], BF16)
        vtm_o = C.dout("vtm_o", [128, 4096], BF16)
        kfm_o = C.dout("kfm_o", [128, 4096], BF16)

    xn1 = C.sbuf("c_xn1", [128, 8 * TG], BF16)
    x1h = C.sbuf("c_x1h", [128, 8 * TG], BF16)
    w_k = C.sbuf("c_wk", [128, 8 * 1024], BF16)
    w_v = C.sbuf("c_wv", [128, 8 * 1024], BF16)
    w_q = C.sbuf("c_wq", [128, 8 * 1024], BF16)
    w1a = C.sbuf("c_w1a", [128, 8 * 512], BF16)
    w2s = C.sbuf("c_w2s", [128, 4 * 64], BF16)
    awte = C.sbuf("c_awte", [128, 8 * 48], BF16)
    ktm = C.sbuf("c_ktm", [128, 4 * 1024], BF16)
    kfm = C.sbuf("c_kfm", [128, 8 * 512], BF16)
    vtm = C.sbuf("c_vtm", [128, 4 * 1024], BF16)
    pre1a = C.sbuf("c_pre1a", [128, 4 * 1024], BF16)
    dha = C.sbuf("c_dha", [128, 4 * 1024], BF16)
    hnb = C.sbuf("c_hnb", [128, 1024], BF16)
    act1f = C.sbuf("c_act1f", [128, 4 * 512], BF16)
    dhf = C.sbuf("c_dhf", [128, 512], BF16)
    ggs = C.sbuf("c_ggs", [128, 256], BF16)
    a1tm = C.sbuf("c_a1tm", [128, 256], BF16)
    dh0s = C.sbuf("c_dh0s", [128, 256], BF16)
    g0sb = C.sbuf("c_g0sb", [64, 4096], F32)
    g1sb = C.sbuf("c_g1sb", [128, 2048], F32)
    slsb = C.sbuf("c_slsb", [64, 1025], F32)
    aesb = C.sbuf("c_aesb", [48, 1], F32)
    ate = C.sbuf("c_ate", [128, 4 * 48], BF16)
    alb = C.sbuf("c_alb", [128, 4 * 64], BF16)
    hbuf = C.sbuf("c_hbuf", [128, 4 * 512], BF16)
    logs = C.sbuf("c_logs", [128, 64], F32)
    max8 = C.sbuf("c_max8", [128, 8], F32)
    nmx = C.sbuf("c_nmx", [128, 1], F32)
    expb = C.sbuf("c_expb", [128, 64], F32)
    m01 = C.sbuf("c_m01", [128, 64], F32)
    ssum = C.sbuf("c_ssum", [128, 1], F32)
    rsum = C.sbuf("c_rsum", [128, 1], F32)
    xtm = C.sbuf("c_xtm", [128, 1025], BF16)
    mw0 = C.sbuf("c_mw0", [128, 256], BF16)
    mw1 = C.sbuf("c_mw1", [128, 2 * 64], BF16)
    mw1t = C.sbuf("c_mw1t", [128, 256], BF16)
    sq1 = C.sbuf("c_sq1", [128, 1024], BF16)
    rstd_row = C.sbuf("c_rstdrow", [1, 512], F32)
    rstdc = C.sbuf("c_rstdc", [128, 4], F32)
    rstdb = C.sbuf("c_rstdb", [128, 512], BF16)
    msk = C.sbuf("c_msk", [128, 16], F32)
    rrk = C.sbuf("c_rrk", [128, 16], F32)
    rr2 = C.sbuf("c_rr2", [128, 16], F32)
    dyc = C.sbuf("c_dyc", [128, 16], F32)
    th2 = C.sbuf("c_th2", [128, 16], F32)
    wkrep = C.sbuf("c_wkrep", [128, 64], BF16)
    wqcol = C.sbuf("c_wqcol", [128, 1], F32)
    snc = C.sbuf("c_snc", [128, 8], F32)
    rnc = C.sbuf("c_rnc", [128, 8], F32)
    biasr = C.sbuf("c_biasr", [128, 48], F32)
    rqf = C.sbuf("c_rqf", [16, 512], F32)
    rq16 = C.sbuf("c_rq16", [16, 512], BF16)
    sel16 = C.sbuf("c_sel16", [128, 8 * 16], BF16)
    expd = C.sbuf("c_expd", [16, 8 * 128], BF16)
    rb1c = C.sbuf("c_rb1c", [128, 4], F32)
    rtau = C.sbuf("c_rtau", [128, 1], F32)
    b2row = C.sbuf("c_b2row", [1, 64], BF16)
    crow = C.sbuf("c_crow", [1, 1024], F32)
    ccols = C.sbuf("c_ccols", [128, 8], BF16)
    tmp1 = C.sbuf("c_tmp1", [1, 512], F32)
    sli = C.sbuf("c_sli", [64, 1024], BF16)
    b1row = C.sbuf("c_b1row", [1, 512], F32)
    tausb = C.sbuf("c_tausb", [1, 1], F32)
    knrow = C.sbuf("c_knrow", [1, 64], F32)
    qnrow = C.sbuf("c_qnrow", [1, 64], F32)
    abrow = C.sbuf("c_abrow", [1, 48], F32)
    ln_rows = {nm: C.sbuf(f"c_lnr_{nm}", [1, 512], F32)[:] for nm in
               ("m", "q", "v", "v2", "sd", "rstd", "mrstd")}
    lntmp = C.sbuf("c_lntmp", [128, 512], BF16)
    if debug:
        dbgH = C.sbuf("c_dbgH", [128, 128], F32)
        dbgJ = C.sbuf("c_dbgJ", [128, 160], F32)
        C.v.memset(dbgH[:], 0.0)
        C.v.memset(dbgJ[:], 0.0)
    P_a = C.psum("P_a", [128, 512])
    P_b = C.psum("P_b", [128, 512])
    P_c = C.psum("P_c", [128, 512])
    P_r1 = C.psum("P_r1", [1, 512])
    P_r2 = C.psum("P_r2", [16, 512])
    P_g0a = C.psum("P_g0a", [64, 256])
    P_g0b = C.psum("P_g0b", [64, 256])
    P_g1 = C.psum("P_g1", [128, 256])
    P_cb = P_c.bitcast(BF16)
    cm = C.std_consts()
    t, v, s, y, g = C.t, C.v, C.s, C.y, C.g
    for nm in ("mm", "ev", "sc", "vx", "tr", "tre", "dx_ev", "ga", "ae",
               "knx", "fold", "w0d", "lgm", "al", "slm", "qm", "qe", "rse"):
        C.n.setdefault(nm, 0)

    g.dma(w_k[:].rearrange("p (k n) -> p k n", k=8), wk_d.rearrange("(k p) n -> p k n", p=128))
    g.dma(w_v[:].rearrange("p (k n) -> p k n", k=8), wv_d.rearrange("(k p) n -> p k n", p=128))
    g.dma(w_q[:].rearrange("p (k n) -> p k n", k=8), wq_d.rearrange("(k p) n -> p k n", p=128))
    g.dma(w1a[:].rearrange("p (k n) -> p k n", k=8),
          w1_d.rearrange("(k p) n -> p k n", p=128)[:, 0:8, :])
    g.dma(w2s[:].rearrange("p (k n) -> p k n", k=4), w2_d.rearrange("(k p) n -> p k n", p=128))
    g.dma(mw0[0:64, :], mw0_d)
    g.dma(mw1[:].rearrange("p (k n) -> p k n", k=2), mw1_d.rearrange("(k p) n -> p k n", p=128))
    g.dma(awte[:].rearrange("p (k n) -> p k n", k=8)[:, :, 0:16], aw_d.rearrange("(k p) n -> p k n", p=128))
    g.dma(awte[:].rearrange("p (k n) -> p k n", k=8)[:, :, 16:32], tw_d.rearrange("(k p) n -> p k n", p=128))
    g.dma(awte[:].rearrange("p (k n) -> p k n", k=8)[:, :, 32:48], ew_d.rearrange("(k p) n -> p k n", p=128))
    g.dma(sli[:], sli_d)
    g.dma(b1row[:], b1_d.rearrange("(a n) -> a n", a=1))
    g.dma(b2row[:], b2_d.rearrange("(a n) -> a n", a=1))
    g.dma(tausb[:], tau_d.rearrange("(a b) -> a b", a=1))
    g.dma(knrow[:], knw_d.rearrange("(a n) -> a n", a=1))
    g.dma(qnrow[:], qnw_d.rearrange("(a n) -> a n", a=1))
    g.dma(abrow[0:1, 0:16], ab_d.rearrange("(a n) -> a n", a=1))
    g.dma(abrow[0:1, 16:32], tb_d.rearrange("(a n) -> a n", a=1))
    g.dma(abrow[0:1, 32:48], eb_d.rearrange("(a n) -> a n", a=1))
    import ml_dtypes as _md
    sel16_np = np.zeros((128, 8 * 16), _md.bfloat16)
    for c8 in range(8):
        sel16_np[0:64, c8 * 16 + 2 * c8] = 1
        sel16_np[64:128, c8 * 16 + 2 * c8 + 1] = 1
    g.dma(sel16[:], C.const("k_sel16", sel16_np))
    expd_np = np.zeros((16, 8 * 128), _md.bfloat16)
    for ch in range(8):
        for hloc in range(2):
            h = 2 * ch + hloc
            expd_np[h, ch * 128 + hloc * 64: ch * 128 + (hloc + 1) * 64] = 1
    g.dma(expd[:], C.const("k_expd", expd_np))
    C.barrier()
    v.memset(g0sb[:], 0.0)
    v.memset(g1sb[:], 0.0)
    v.memset(slsb[:], 0.0)
    v.memset(aesb[:], 0.0)
    v.memset(xtm[:, 1024:1025], 1.0)
    v.inc("ev")
    for _m in vecs_to_cols(C, [(snw_d, snc[:]), (rnw_d, rnc[:])], P_a[:, 0:8]):
        v.wait(*_m)
    for k in range(8):
        v.tensor_scalar(w_k[:, k * 1024:(k + 1) * 1024], w_k[:, k * 1024:(k + 1) * 1024],
                        snc[:, k:k + 1], None, ALU.mult)
        v.tensor_scalar(w_v[:, k * 1024:(k + 1) * 1024], w_v[:, k * 1024:(k + 1) * 1024],
                        snc[:, k:k + 1], None, ALU.mult)
        v.tensor_scalar(w_q[:, k * 1024:(k + 1) * 1024], w_q[:, k * 1024:(k + 1) * 1024],
                        rnc[:, k:k + 1], None, ALU.mult)
        v.tensor_scalar(awte[:, k * 48:(k + 1) * 48], awte[:, k * 48:(k + 1) * 48],
                        snc[:, k:k + 1], None, ALU.mult)
    v.tensor_copy(mw0[64:128, :], mw0[0:64, :])
    v.inc("fold")
    t.wait("fold", C.n["fold"])
    for hc in range(2):
        t.wait("tre", C.n["tr"])
        t.transpose(P_cb[0:64, 0:128], mw1[:, hc * 64:(hc + 1) * 64], cm["ident_b"][:])
        t.inc("tr")
        v.wait("tr", C.n["tr"])
        v.tensor_copy(mw1t[0:64, hc * 128:(hc + 1) * 128], P_cb[0:64, 0:128])
        v.inc("tre")
    t.wait("tre", C.n["tre"])
    v.tensor_copy(mw1t[64:128, :], mw1t[0:64, :])
    t.matmul(P_a[:, 0:64], cm["ones1_f"][:], knrow[:], start=True, stop=True)
    t.inc("mm")
    v.wait("mm", C.n["mm"])
    v.tensor_copy(wkrep[:], P_a[:, 0:64])
    v.inc("ev")
    t.wait("ev", C.n["ev"])
    t.matmul(P_b[0:64, 0:1], qnrow[:], cm["ones11_f"][:], start=True, stop=True)
    t.inc("mm")
    v.wait("mm", C.n["mm"])
    v.tensor_copy(wqcol[0:64, :], P_b[0:64, 0:1])
    v.tensor_copy(wqcol[64:128, :], P_b[0:64, 0:1])
    v.inc("ev")
    t.wait("ev", C.n["ev"])
    t.matmul(P_a[:, 0:48], cm["ones1_f"][:], abrow[:], start=True, stop=True)
    t.inc("mm")
    v.wait("mm", C.n["mm"])
    v.tensor_copy(biasr[:], P_a[:, 0:48])
    v.inc("ev")
    t.wait("ev", C.n["ev"])
    t.matmul(P_b[:, 0:1], cm["ones1_f"][:], tausb[:], start=True, stop=True)
    t.inc("mm")
    s.wait("mm", C.n["mm"])
    s.activation(out=rtau[:], in_=P_b[:, 0:1], func=AF.Abs)
    s.inc("sc")
    v.wait("sc", C.n["sc"])
    v.tensor_scalar(rtau[:], rtau[:], 0.1, None, ALU.add)
    recip_rows(C, rtau[:], rtau[:], "dx_ev")
    v.inc("ev")
    t.wait("ev", C.n["ev"])
    for h5 in range(2):
        t.matmul(P_r1[:], cm["onesc_b"][0:64, :], sli[:, h5 * 512:(h5 + 1) * 512],
                 start=True, stop=True)
        t.inc("mm")
        v.wait("mm", C.n["mm"])
        v.tensor_scalar(crow[0:1, h5 * 512:(h5 + 1) * 512], P_r1[0:1, :],
                        1.0 / 64.0, None, ALU.mult)
        v.inc("ev")
        t.wait("ev", C.n["ev"])
    row_to_cols(C, crow[0:1, :], 8, rstdc[:, 0:4], P_a[:, 0:8], "ccol") if False else None
    row_to_cols(C, crow[0:1, :], 8, msk[:, 0:8], P_a[:, 0:8], "ccol")
    v.tensor_copy(ccols[:], msk[:, 0:8])
    v.inc("ev")
    for k in range(8):
        mkk = g.dma(sq1[:, 0:512], w1_d.rearrange("(k p) n -> p k n", p=128)[:, 8 + k, :], tag="w1b")
        t.wait(*mkk)
        if k > 0:
            t.wait("ev", C.n["ev"])
        t.matmul(P_r1[:], ccols[:, k:k + 1], sq1[:, 0:512], start=(k == 0), stop=(k == 7))
        t.inc("mm")
        if k < 7:
            g.wait("mm", C.n["mm"])
    v.wait("mm", C.n["mm"])
    v.tensor_tensor(tmp1[:], P_r1[0:1, :], b1row[:], ALU.add)
    v.inc("ev")
    t.wait("ev", C.n["ev"])
    row_to_cols(C, tmp1[0:1, :], 4, rb1c[:, 0:4], P_a[:, 0:8], "rb1c")
    C.barrier()

    lbuf = {"sq": sq1[:], "tmp": lntmp[:],
            "rows_m": ln_rows["m"], "rows_q": ln_rows["q"], "rows_v": ln_rows["v"],
            "rows_v2": ln_rows["v2"], "rows_sd": ln_rows["sd"],
            "rows_rstd": ln_rows["rstd"], "rows_mrstd": ln_rows["mrstd"],
            "ps_sum": P_r1[:], "ps_sq": P_r1[:] if False else None,
            "ps_b1": P_a[:], "ps_b2": P_b[:]}
    lbuf["ps_sq"] = P_r2[0:1, :]
    qdma_mark = None
    for tg in range(8):
        lo = tg * TG
        if qdma_mark is not None:
            g.wait(*qdma_mark)
        xm = g.dma(x1h[:].rearrange("p (c u) -> p c u", c=8),
                   x1_d.rearrange("(c p) u -> p c u", p=128)[:, :, lo:lo + TG], tag="dx")
        t.wait(*xm)
        v.wait(*xm)
        s.wait(*xm)
        ln_plain(C, x1h[:], xn1[:], TG, lbuf)
        s.wait("ln_eb", C.n["ln_eb"])
        base_sc = C.n["sc"]
        for c8 in range(8):
            if c8 >= 2:
                s.wait("mm", C.n["mm"] - 1)
            s.activation(out=sq1[:, (c8 % 2) * 512:(c8 % 2 + 1) * 512],
                         in_=xn1[:, c8 * TG:(c8 + 1) * TG], func=AF.Square)
            s.inc("sc")
            t.wait("sc", base_sc + c8 + 1)
            t.matmul(P_r1[:], cm["onesc_b"][:], sq1[:, (c8 % 2) * 512:(c8 % 2 + 1) * 512],
                     start=(c8 == 0), stop=(c8 == 7))
            t.inc("mm")
        v.wait("mm", C.n["mm"])
        v.tensor_scalar(rstd_row[:], P_r1[0:1, :], 1.0 / 1024.0, 1e-6, ALU.mult, ALU.add)
        v.inc("ev")
        s.wait("ev", C.n["ev"])
        s.activation(out=rstd_row[:], in_=rstd_row[:], func=AF.Sqrt)
        s.inc("sc")
        v.wait("sc", C.n["sc"])
        recip_rows(C, rstd_row[:], rstd_row[:], "dx_ev")
        v.inc("ev")
        t.wait("ev", C.n["ev"])
        row_to_cols(C, rstd_row[0:1, :], 4, rstdc[:], P_a[:, 0:8], "rsc")
        t.matmul(P_b[:], cm["ones1_f"][:], rstd_row[:], start=True, stop=True)
        t.inc("mm")
        v.wait("mm", C.n["mm"])
        v.tensor_copy(rstdb[:], P_b[:])
        v.inc("rse")
        t.wait("rse", C.n["rse"])
        for tc in range(4):
            for which, wsb, dst in (("k", w_k, ktm), ("v", w_v, vtm)):
                for nh in range(2):
                    ps = (P_a, P_b)[nh]
                    mm_acc(C, ps[:], [(xn1[:, k * TG + tc * 128: k * TG + (tc + 1) * 128],
                                       wsb[:, k * 1024 + nh * 512: k * 1024 + nh * 512 + 512])
                                      for k in range(8)], "mm")
                    s.wait("mm", C.n["mm"])
                    s.activation(out=dst[:, tc * 1024 + nh * 512: tc * 1024 + nh * 512 + 512],
                                 in_=ps[:], func=AF.Copy, scale=rstdc[:, tc:tc + 1])
                    s.inc("sc")
                    t.wait("sc", C.n["sc"])
            s.activation(out=sq1[:], in_=ktm[:, tc * 1024:(tc + 1) * 1024], func=AF.Square)
            s.inc("sc")
            v.wait("sc", C.n["sc"])
            for h16 in range(16):
                v.reduce_sum(msk[:, h16:h16 + 1], sq1[:, h16 * 64:(h16 + 1) * 64], axis=AX.X)
            drain(C)
            v.tensor_scalar(msk[:], msk[:], 1.0 / 64.0, 1e-6, ALU.mult, ALU.add)
            v.inc("ev")
            s.wait("ev", C.n["ev"])
            s.activation(out=rrk[:], in_=msk[:], func=AF.Sqrt)
            s.inc("sc")
            v.wait("sc", C.n["sc"])
            recip_rows(C, rrk[:], rrk[:], "dx_ev")
            v.tensor_tensor(hnb[:].rearrange("p (a b) -> p a b", a=16),
                            ktm[:, tc * 1024:(tc + 1) * 1024].rearrange("p (a b) -> p a b", a=16),
                            rrk[:, :, None].broadcast_to([128, 16, 64]), ALU.mult)
            v.tensor_tensor(ktm[:, tc * 1024:(tc + 1) * 1024].rearrange("p (a b) -> p a b", a=16),
                            hnb[:].rearrange("p (a b) -> p a b", a=16),
                            wkrep[:, None, :].broadcast_to([128, 16, 64]), ALU.mult)
            v.inc("knx")
            t.wait("knx", C.n["knx"])
            for c8 in range(8):
                t.wait("tre", C.n["tr"])
                t.transpose(P_cb[:, 0:128], ktm[:, tc * 1024 + c8 * 128: tc * 1024 + (c8 + 1) * 128],
                            cm["ident_b"][:])
                t.inc("tr")
                v.wait("tr", C.n["tr"])
                v.tensor_copy(kfm[:, c8 * TG + tc * 128: c8 * TG + (tc + 1) * 128], P_cb[:, 0:128])
                v.inc("tre")
            mm_acc(C, P_a[:, 0:48], [(xn1[:, k * TG + tc * 128: k * TG + (tc + 1) * 128],
                                      awte[:, k * 48:(k + 1) * 48]) for k in range(8)], "mm")
            v.wait("mm", C.n["mm"])
            v.tensor_scalar(lntmp[:, 0:48], P_a[:, 0:48], rstdc[:, tc:tc + 1], None, ALU.mult)
            v.tensor_copy(C.dmb[:], C.dmb[:])
            v.tensor_tensor(lntmp[:, 48:96], lntmp[:, 0:48], biasr[:], ALU.add)
            v.inc("ev")
            s.wait("ev", C.n["ev"])
            s.activation(out=ate[:, tc * 48:(tc + 1) * 48], in_=lntmp[:, 48:96], func=AF.Sigmoid)
            s.inc("sc")
            t.wait("sc", C.n["sc"])
            t.matmul(P_r2[0:48, 0:1].rearrange("a b -> a b") if False else P_g1[0:48, 0:1],
                     ate[:, tc * 48:(tc + 1) * 48], cm["onesc_b"][:], start=True, stop=True)
            t.inc("mm")
            v.wait("mm", C.n["mm"])
            v.tensor_tensor(aesb[:], aesb[:], P_g1[0:48, 0:1], ALU.add)
            v.inc("ae")
            t.wait("ae", C.n["ae"])
        if debug and tg == 0:
            y.wait("knx", C.n["knx"])
            y.wait("ev", C.n["ev"])
            y.wait("tre", C.n["tre"])
            y.dma(ktm_o, ktm[:], tag="dbg")
            y.dma(vtm_o, vtm[:], tag="dbg")
            y.dma(kfm_o, kfm[:], tag="dbg")
        C.barrier()
        for hp in range(8):
            hA, hB = 2 * hp, 2 * hp + 1
            for hl, h in ((0, hA), (1, hB)):
                rb_ = (h % 2) * 64
                ch_ = h // 2
                for mc in range(2):
                    ps = (P_a, P_b)[mc]
                    t.wait("ev", C.n["ev"])
                    t.matmul(ps[:], mw0[rb_:rb_ + 64, mc * 128:(mc + 1) * 128],
                             kfm[rb_:rb_ + 64, ch_ * TG: ch_ * TG + TG], start=True, stop=True)
                    t.inc("mm")
                    s.wait("mm", C.n["mm"])
                    s.activation(out=act1f[:, (hl * 2 + mc) * 512:(hl * 2 + mc + 1) * 512],
                                 in_=ps[:], func=AF.Gelu)
                    s.inc("sc")
                    C.ops["scalar"].append(("wait", C.semname("sc"), C.n["sc"]))
                    v.wait("sc", C.n["sc"])
                    v.tensor_copy(lntmp[0:1, 0:1], lntmp[0:1, 0:1])
                    v.inc("ev")
            t.wait("sc", C.n["sc"])
            for tc in range(4):
                t.wait("qe", C.n["qe"])
                for hl, h in ((0, hA), (1, hB)):
                    for hc in range(2):
                        t.matmul(P_c[:, hl * 64:(hl + 1) * 64],
                                 act1f[:, (hl * 2 + hc) * 512 + tc * 128: (hl * 2 + hc) * 512 + (tc + 1) * 128],
                                 mw1[:, hc * 64:(hc + 1) * 64], start=(hc == 0), stop=(hc == 1))
                t.inc("mm")
                v.wait("mm", C.n["mm"])
                v.tensor_copy(pre1a[:, tc * 1024 + hA * 64: tc * 1024 + (hB + 1) * 64], P_c[:, 0:128])
                v.inc("qe")
        if debug and tg == 0:
            y.wait("qe", C.n["qe"])
            y.dma(pre1a_o, pre1a[:], tag="dbg")
        C.barrier()
        for tc in range(4):
            s.wait("ev", C.n["ev"])
            s.activation(out=sq1[:], in_=pre1a[:, tc * 1024:(tc + 1) * 1024], func=AF.Square)
            s.inc("sc")
            v.wait("sc", C.n["sc"])
            for h16 in range(16):
                v.reduce_sum(msk[:, h16:h16 + 1], sq1[:, h16 * 64:(h16 + 1) * 64], axis=AX.X)
            drain(C)
            v.tensor_scalar(msk[:], msk[:], 1.0 / 64.0, 1e-8, ALU.mult, ALU.add)
            if debug and tg == 0:
                v.tensor_copy(dbgH[:, 64 + tc * 16: 64 + (tc + 1) * 16], msk[:])
            v.inc("ev")
            s.wait("ev", C.n["ev"])
            s.activation(out=rr2[:], in_=msk[:], func=AF.Sqrt)
            s.inc("sc")
            v.wait("sc", C.n["sc"])
            recip_rows(C, rr2[:], rr2[:], "dx_ev")
            if debug and tg == 0:
                v.tensor_copy(dbgH[:, tc * 16:(tc + 1) * 16], rr2[:])
            pv = pre1a[:, tc * 1024:(tc + 1) * 1024].rearrange("p (a b) -> p a b", a=16)
            kv = ktm[:, tc * 1024:(tc + 1) * 1024].rearrange("p (a b) -> p a b", a=16)
            vv = vtm[:, tc * 1024:(tc + 1) * 1024].rearrange("p (a b) -> p a b", a=16)
            dv = dha[:, tc * 1024:(tc + 1) * 1024].rearrange("p (a b) -> p a b", a=16)
            sv = sq1[:].rearrange("p (a b) -> p a b", a=16)
            hv = hnb[:].rearrange("p (a b) -> p a b", a=16)
            rr2v = rr2[:, :, None].broadcast_to([128, 16, 64])
            v.tensor_scalar(th2[:], ate[:, tc * 48 + 16: tc * 48 + 32], 0.02, None, ALU.mult)
            v.tensor_tensor(hv, pv, rr2v, ALU.mult)
            v.tensor_tensor(sv, hv, kv, ALU.add)
            v.tensor_tensor(sv, sv, vv, ALU.subtract)
            v.tensor_tensor(sv, sv, th2[:, :, None].broadcast_to([128, 16, 64]), ALU.mult)
            v.tensor_tensor(dv, sv, hv, ALU.mult)
            drain(C)
            for h16 in range(16):
                v.reduce_sum(dyc[:, h16:h16 + 1],
                             dha[:, tc * 1024 + h16 * 64: tc * 1024 + (h16 + 1) * 64], axis=AX.X)
            drain(C)
            v.tensor_scalar(dyc[:], dyc[:], 1.0 / 64.0, None, ALU.mult)
            drain(C)
            v.tensor_tensor(hv, hv, dyc[:, :, None].broadcast_to([128, 16, 64]), ALU.mult)
            v.tensor_tensor(sv, sv, hv, ALU.subtract)
            v.tensor_tensor(dv, sv, rr2v, ALU.mult)
            v.inc("ev")
        if debug and tg == 0:
            y.wait("ev", C.n["ev"])
            y.dma(dha_o, dha[:], tag="dbg")
            y.dma(dbgH_o, dbgH[:], tag="dbg")
        C.barrier()
        for hp in range(8):
            hA, hB = 2 * hp, 2 * hp + 1
            for tc in range(4):
                t.wait("tre", C.n["tr"])
                t.transpose(P_cb[:, 0:128], dha[:, tc * 1024 + hA * 64: tc * 1024 + (hB + 1) * 64],
                            cm["ident_b"][:])
                t.inc("tr")
                v.wait("tr", C.n["tr"])
                v.tensor_copy(dhf[:, tc * 128:(tc + 1) * 128], P_cb[:, 0:128])
                v.inc("tre")
            for tc in range(4):
                for hl, h in ((0, hA), (1, hB)):
                    rb_ = (h % 2) * 64
                    ch_ = h // 2
                    t.wait("ga", C.n["ga"])
                    t.matmul(P_a[:, 0:256], kfm[rb_:rb_ + 64, ch_ * TG + tc * 128: ch_ * TG + (tc + 1) * 128],
                             mw0[rb_:rb_ + 64, :], start=True, stop=True)
                    t.inc("mm")
                    s.wait("mm", C.n["mm"])
                    s.activation(out=ggs[:], in_=P_a[:, 0:256], func=AF.Derivative_Gelu)
                    s.inc("sc")
                    s.activation(out=a1tm[:], in_=P_a[:, 0:256], func=AF.Gelu)
                    s.inc("sc")
                    t.matmul(P_b[:, 0:256], dhf[rb_:rb_ + 64, tc * 128:(tc + 1) * 128],
                             mw1t[rb_:rb_ + 64, :], start=True, stop=True)
                    t.inc("mm")
                    v.wait("mm", C.n["mm"])
                    v.wait("sc", C.n["sc"])
                    v.tensor_tensor(dh0s[:], P_b[:, 0:256], ggs[:], ALU.mult)
                    v.inc("ga")
                    t.wait("ga", C.n["ga"])
                    t.matmul((P_g0a, P_g0b)[hl][:],
                             ktm[:, tc * 1024 + h * 64: tc * 1024 + (h + 1) * 64],
                             dh0s[:], start=(tc == 0), stop=(tc == 3))
                    t.inc("mm")
                    for hc in range(2):
                        t.matmul(P_g1[:, (hl * 2 + hc) * 64:(hl * 2 + hc + 1) * 64],
                                 a1tm[:, hc * 128:(hc + 1) * 128],
                                 dha[:, tc * 1024 + h * 64: tc * 1024 + (h + 1) * 64],
                                 start=(tc == 0 and hl == 0 and hc == 0),
                                 stop=(tc == 3 and hl == 1 and hc == 1))
                    t.inc("mm")
                    v.wait("mm", C.n["mm"])
                    v.tensor_copy(lntmp[0:1, 0:1], lntmp[0:1, 0:1])
                    v.inc("ga")
            v.wait("mm", C.n["mm"])
            v.tensor_tensor(g0sb[:, hA * 256:(hA + 1) * 256], g0sb[:, hA * 256:(hA + 1) * 256],
                            P_g0a[:], ALU.add)
            v.tensor_tensor(g0sb[:, hB * 256:(hB + 1) * 256], g0sb[:, hB * 256:(hB + 1) * 256],
                            P_g0b[:], ALU.add)
            v.tensor_tensor(g1sb[:, hA * 128:(hB + 1) * 128], g1sb[:, hA * 128:(hB + 1) * 128],
                            P_g1[:], ALU.add)
            v.inc("ga")
            t.wait("ga", C.n["ga"])
        C.barrier()
        for hm in range(4):
            ps = (P_a, P_b)[hm % 2]
            if hm >= 2:
                t.wait("ev", C.n["ev"] - 1)
            mm_acc(C, ps[:], [(xn1[:, k * TG: k * TG + 128].rearrange("p n -> p n") if False else
                               w1a[:, k * 512 + hm * 128: k * 512 + (hm + 1) * 128],
                               xn1[:, k * TG: (k + 1) * TG]) for k in range(8)], "mm")
            s.wait("mm", C.n["mm"])
            s.activation(out=hbuf[:, hm * 512:(hm + 1) * 512], in_=ps[:], func=AF.Gelu,
                         bias=rb1c[:, hm:hm + 1])
            s.inc("sc")
            v.wait("sc", C.n["sc"])
            v.tensor_copy(lntmp[0:1, 0:1], lntmp[0:1, 0:1])
            v.inc("ev")
        for tc in range(4):
            t.wait("sc", C.n["sc"])
            t.wait("al", C.n["al"])
            for hc in range(4):
                t.matmul(P_c[:, 0:64], hbuf[:, hc * 512 + tc * 128: hc * 512 + (tc + 1) * 128],
                         w2s[:, hc * 64:(hc + 1) * 64], start=(hc == 0), stop=False)
            t.matmul(P_c[:, 0:64], cm["ones1_b"][:, tc * 0: 128] if False else cm["ones1_b"][:],
                     b2row[:], start=False, stop=True)
            t.inc("lgm")
            v.wait("lgm", C.n["lgm"])
            v.tensor_scalar(logs[:], P_c[:, 0:64], rtau[:, 0:1], None, ALU.mult)
            v.inc("dx_ev")
            v.wait("dx_ev", C.n["dx_ev"])
            v.max(out=max8[:], in_=logs[:])
            v.inc("dx_ev")
            v.wait("dx_ev", C.n["dx_ev"])
            if debug and tg == 0 and tc == 0:
                v.tensor_copy(dbgJ[:, 0:64], logs[:])
                v.tensor_copy(dbgJ[:, 64:72], max8[:])
            v.tensor_copy(C.dmb[:], C.dmb[:])
            v.tensor_scalar(nmx[:], max8[:, 0:1], -1.0, None, ALU.mult)
            if debug and tg == 0 and tc == 0:
                v.tensor_copy(dbgJ[:, 72:73], nmx[:])
            v.inc("ev")
            s.wait("ev", C.n["ev"])
            s.activation(out=expb[:], in_=logs[:], func=AF.Exp, bias=nmx[:, 0:1])
            s.inc("sc")
            v.tensor_tensor(m01[:], logs[:], max8[:, 7:8].broadcast_to([128, 64]), ALU.is_ge)
            v.wait("sc", C.n["sc"])
            v.tensor_tensor(expb[:], expb[:], m01[:], ALU.mult)
            drain(C)
            v.reduce_sum(ssum[:], expb[:], axis=AX.X)
            recip_rows(C, rsum[:], ssum[:], "dx_ev")
            if debug and tg == 0 and tc == 0:
                v.tensor_copy(dbgJ[:, 80:144], expb[:])
                v.tensor_copy(dbgJ[:, 144:145], ssum[:])
                v.tensor_copy(dbgJ[:, 145:146], rsum[:])
                v.tensor_copy(dbgJ[:, 146:210] if False else dbgJ[:, 146:147], m01[:, 0:1])
            v.tensor_scalar(alb[:, tc * 64:(tc + 1) * 64], expb[:], rsum[:, 0:1], None, ALU.mult)
            v.inc("al")
            t.wait("al", C.n["al"])
            for c8 in range(8):
                t.wait("tre", C.n["tr"])
                t.transpose(P_cb[:, 0:128], xn1[:, c8 * TG + tc * 128: c8 * TG + (tc + 1) * 128],
                            cm["ident_b"][:])
                t.inc("tr")
                v.wait("tr", C.n["tr"])
                v.tensor_copy(xtm[:, c8 * 128:(c8 + 1) * 128], P_cb[:, 0:128])
                v.inc("tre")
            t.wait("tre", C.n["tre"])
            t.matmul(P_a[0:64, :], alb[:, tc * 64:(tc + 1) * 64],
                     xtm[:, 0:512], start=(tc == 0), stop=(tc == 3))
            t.inc("slm")
            t.matmul(P_b[0:64, :], alb[:, tc * 64:(tc + 1) * 64], xtm[:, 512:1024],
                     start=(tc == 0), stop=(tc == 3))
            t.inc("slm")
            t.matmul(P_g1[0:64, 0:1], alb[:, tc * 64:(tc + 1) * 64], xtm[:, 1024:1025],
                     start=(tc == 0), stop=(tc == 3))
            t.inc("slm")
        v.wait("slm", C.n["slm"])
        v.tensor_tensor(slsb[:, 0:512], slsb[:, 0:512], P_a[0:64, :], ALU.add)
        v.tensor_tensor(slsb[:, 512:1024], slsb[:, 512:1024], P_b[0:64, :], ALU.add)
        v.tensor_tensor(slsb[:, 1024:1025], slsb[:, 1024:1025], P_g1[0:64, 0:1], ALU.add)
        v.inc("slm")
        t.wait("slm", C.n["slm"])
        y.wait("al", C.n["al"])
        am = y.dma(alpha_o[lo:lo + TG, :].rearrange("(a p) n -> p a n", p=128),
                   alb[:].rearrange("p (a n) -> p a n", a=4), tag="da")
        if debug and tg == 0:
            y.wait("sc", C.n["sc"])
            y.dma(dbgJ_o, dbgJ[:], tag="dbg")
            y.dma(hbuf_o, hbuf[:], tag="dbg")
        C.barrier()
        for m8 in range(8):
            ps = (P_a, P_b)[m8 % 2]
            if m8 >= 2:
                t.wait("qm", C.n["qm"] - 1)
            mm_acc(C, ps[:], [(w_q[:, k * 1024 + m8 * 128: k * 1024 + (m8 + 1) * 128],
                               xn1[:, k * TG:(k + 1) * TG]) for k in range(8)], "mm")
            v.wait("mm", C.n["mm"])
            v.tensor_tensor(x1h[:, m8 * TG:(m8 + 1) * TG], ps[:], rstdb[:], ALU.mult)
            v.inc("qm")
        for c8 in range(8):
            s.wait("qm", C.n["qm"])
            s.activation(out=sq1[:, 0:512], in_=x1h[:, c8 * TG:(c8 + 1) * TG], func=AF.Square)
            s.inc("sc")
            t.wait("sc", C.n["sc"])
            if c8 == 0:
                t.wait("qe", C.n["qe"])
            t.matmul(P_r2[0:16, :], sel16[:, c8 * 16:(c8 + 1) * 16], sq1[:, 0:512],
                     start=(c8 == 0), stop=(c8 == 7))
            t.inc("mm")
            s.wait("mm", C.n["mm"])
        v.wait("mm", C.n["mm"])
        v.tensor_scalar(rqf[:], P_r2[0:16, :], 1.0 / 64.0, 1e-6, ALU.mult, ALU.add)
        v.inc("ev")
        s.wait("ev", C.n["ev"])
        s.activation(out=rqf[:], in_=rqf[:], func=AF.Sqrt)
        s.inc("sc")
        v.wait("sc", C.n["sc"])
        recip_rows(C, rqf[:], rqf[:], "dx_ev")
        v.tensor_copy(rq16[:], rqf[:])
        v.inc("ev")
        t.wait("ev", C.n["ev"])
        for c8 in range(8):
            t.matmul(P_c[:], expd[:, c8 * 128:(c8 + 1) * 128], rq16[:], start=True, stop=True)
            t.inc("mm")
            v.wait("mm", C.n["mm"])
            v.tensor_tensor(sq1[:, 0:512], x1h[:, c8 * TG:(c8 + 1) * TG], P_c[:], ALU.mult)
            v.tensor_scalar(x1h[:, c8 * TG:(c8 + 1) * TG], sq1[:, 0:512], wqcol[:, 0:1], None, ALU.mult)
            v.inc("qe")
            t.wait("qe", C.n["qe"])
        y.wait("qe", C.n["qe"])
        qdma_mark = y.dma(q_o.rearrange("(c p) u -> p c u", p=128)[:, :, lo:lo + TG],
                          x1h[:].rearrange("p (c u) -> p c u", c=8), tag="dq")
        C.barrier()
    y.dma(g0_o, g0sb[:])
    y.dma(g1_o, g1sb[:])
    y.dma(slot_o, slsb[:])
    y.dma(ae_o, aesb[:])
    C.barrier()
    return C


def build_d(race=True, C=None, io=None):
    if C is None:
        C = Ctx("d", race=race)
    T = 4096
    TG = 512
    if io is None:
        io = {"x1": C.din("x1", [1024, T], BF16),
              "q": C.din("q", [1024, T], BF16),
              "af": C.din("af", [T, 128], BF16),
              "g0a": C.din("g0a", [64, 4096], F32),
              "g0b": C.din("g0b", [64, 4096], F32),
              "g1a": C.din("g1a", [128, 2048], F32),
              "g1b": C.din("g1b", [128, 2048], F32),
              "aea": C.din("aea", [48, 1], F32),
              "aeb": C.din("aeb", [48, 1], F32),
              "sla": C.din("sla", [64, 1025], F32),
              "slb": C.din("slb", [64, 1025], F32),
              "mW0": C.din("mW0", [64, 256], F32),
              "mW1": C.din("mW1", [256, 64], F32),
              "slot_init": C.din("slot_init", [64, 1024], F32),
              "Wih": C.din("Wih", [3072, 1024], F32),
              "Whh": C.din("Whh", [3072, 1024], F32),
              "bih": C.din("bih", [3072], F32),
              "bhh": C.din("bhh", [3072], F32),
              "wq": C.din("wq", [1024, 256], F32),
              "bq": C.din("bq", [256], F32),
              "wk": C.din("wk", [1024, 256], F32),
              "bk": C.din("bk", [256], F32),
              "wv": C.din("wv", [1024, 1024], F32),
              "bv": C.din("bv", [1024], F32),
              "lng": C.din("lng", [1024], F32),
              "lnb": C.din("lnb", [1024], F32),
              "wvp": C.din("wvp", [1024, 1024], F32),
              "bvp": C.din("bvp", [1024], F32),
              "wo": C.din("wo", [1024, 1024], F32),
              "bo": C.din("bo", [1024], F32),
              "out_o": C.dout("out_o", [T, 1024], F32)}
    x1_d = io["x1"]; q_d = io["q"]; af_d = io["af"]
    g0a_d = io["g0a"]; g0b_d = io["g0b"]; g1a_d = io["g1a"]; g1b_d = io["g1b"]
    aea_d = io["aea"]; aeb_d = io["aeb"]; sla_d = io["sla"]; slb_d = io["slb"]
    mw0_d = io["mW0"]; mw1_d = io["mW1"]; sli_d = io["slot_init"]
    wih_d = io["Wih"]; whh_d = io["Whh"]; bih_d = io["bih"]; bhh_d = io["bhh"]
    wqf_d = io["wq"]; bqf_d = io["bq"]; wkf_d = io["wk"]; bkf_d = io["bk"]
    wvf_d = io["wv"]; bvf_d = io["bv"]; lng_d = io["lng"]; lnb_d = io["lnb"]
    wvp_d = io["wvp"]; bvp_d = io["bvp"]; wof_d = io["wo"]; bof_d = io["bo"]
    out_o = io["out_o"]
    wih_bf = C.dint(C.prefix + "wih_bf", [3072, 1024], BF16)
    whh_bf = C.dint(C.prefix + "whh_bf", [3072, 1024], BF16)

    arena = C.sbuf("d_arena", [128, 24576], BF16)
    far = C.sbuf("d_far", [128, 4096], F32)
    fbr = C.sbuf("d_fbr", [128, 4096], F32)
    fcr = C.sbuf("d_fcr", [128, 2048], F32)
    nW0b = C.sbuf("d_nW0b", [128, 4096], BF16)
    nW1b = C.sbuf("d_nW1b", [128, 2048], BF16)
    afm = C.sbuf("d_afm", [128, 4096], BF16)
    svwb = C.sbuf("d_svwb", [64, 1024], BF16)
    bofb = C.sbuf("d_bofb", [128, 1024], F32)
    qtg = arena[:, 0:4096]
    x1tg = arena[:, 4096:8192]
    act1rf = arena[:, 8192:10240]
    prer = arena[:, 10240:14336]
    sqd = arena[:, 14336:15360]
    q_tm = arena[:, 15360:16384]
    x1_tm = arena[:, 16384:17408]
    mskd = C.sbuf("d_mskd", [128, 16], F32)
    rrd = C.sbuf("d_rrd", [128, 16], F32)
    wqk = C.sbuf("d_wqk", [128, 4096], BF16)
    sli_f = C.sbuf("d_slif", [64, 1024], F32)
    sliF = C.sbuf("d_sliF", [128, 512], BF16)
    si_b = C.sbuf("d_sib", [64, 1024], BF16)
    siF = C.sbuf("d_siF", [128, 512], BF16)
    gi_t = far[0:64, 0:3072]
    gn_t = far[0:64, 3072:4096]
    gh_t = fbr[0:64, 0:3072]
    Sn_t = fbr[0:64, 3072:4096]
    si_t = fcr[0:64, 0:1024]
    gr_t = fcr[0:64, 1024:2048]
    vs_t = far[0:64, 0:1024]
    y_t = far[0:64, 1024:2048]
    sv_t = far[0:64, 2048:3072]
    x2acc = far[:, 0:4096]
    Sn_b = C.sbuf("d_Snb", [64, 1024], BF16)
    SnF = C.sbuf("d_SnF", [128, 512], BF16)
    qs_b = C.sbuf("d_qsb", [64, 256], BF16)
    ks_b = C.sbuf("d_ksb", [64, 256], BF16)
    qsF = C.sbuf("d_qsF", [128, 128], BF16)
    ksF = C.sbuf("d_ksF", [128, 128], BF16)
    att_t = C.sbuf("d_attt", [64, 64], F32)
    attb = C.sbuf("d_attb", [64, 64], BF16)
    attT = C.sbuf("d_attT", [64, 64], BF16)
    yF = C.sbuf("d_yF", [128, 512], BF16)
    svF = C.sbuf("d_svF", [128, 512], BF16)
    lngb = C.sbuf("d_lngb", [64, 1024], F32)
    lnbb = C.sbuf("d_lnbb", [64, 1024], F32)
    rowA = C.sbuf("d_rowA", [1, 3072], BF16)
    rowB = C.sbuf("d_rowB", [1, 3072], BF16)
    rowq = C.sbuf("d_rowq", [1, 256], F32)
    rowk = C.sbuf("d_rowk", [1, 256], F32)
    rowv = C.sbuf("d_rowv", [1, 1024], F32)
    rowvp = C.sbuf("d_rowvp", [1, 1024], F32)
    rowo = C.sbuf("d_rowo", [1, 1024], F32)
    rowg = C.sbuf("d_rowg", [1, 1024], F32)
    rowb = C.sbuf("d_rowb", [1, 1024], F32)
    aec = C.sbuf("d_aec", [48, 1], F32)
    aecb = C.sbuf("d_aecb", [16, 1], BF16)
    aec2 = C.sbuf("d_aec2", [48, 1], F32)
    sqcol = C.sbuf("d_sqcol", [128, 16], BF16)
    nrow = C.sbuf("d_nrow", [1, 16], F32)
    amr = C.sbuf("d_amr", [1, 16], F32)
    s0r = C.sbuf("d_s0r", [1, 16], F32)
    cfr = C.sbuf("d_cfr", [1, 16], F32)
    s0c = C.sbuf("d_s0c", [128, 16], F32)
    cfc = C.sbuf("d_cfc", [128, 16], F32)
    mw0f = C.sbuf("d_mw0f", [64, 256], F32)
    mw1f = C.sbuf("d_mw1f", [128, 128], F32)
    mu_c = C.sbuf("d_muc", [64, 1], F32)
    var_c = C.sbuf("d_varc", [64, 1], F32)
    rmx_c = C.sbuf("d_rmxc", [64, 1], F32)
    ssm_c = C.sbuf("d_ssmc", [64, 1], F32)
    den_c = C.sbuf("d_denc", [64, 1], F32)

    P_a = C.psum("P_a", [128, 512])
    P_b = C.psum("P_b", [128, 512])
    P_c = C.psum("P_c", [128, 512])
    P_cb = C.psum("P_cb", [128, 128], BF16)
    P_r = C.psum("P_r", [128, 512])

    cm = C.std_consts()
    i16 = C.sbuf("d_i16", [16, 16], BF16)
    t, v, s, y, g = C.t, C.v, C.s, C.y, C.g
    y.dma(i16[:], C.const("k_ident16b", np.eye(16, dtype=ml_dtypes.bfloat16)), tag="dcw")
    for nm in ("mm", "ev", "sc", "tr", "tre", "dx", "drv", "w0e", "w1e", "gie",
               "qke", "vse", "ye", "lge", "sve", "swe", "boe", "fe", "pe",
               "f1m", "f1e"):
        C.n.setdefault(nm, 0)

    g.dma(wih_bf.ap(), wih_d, tag="wcast")
    g.dma(whh_bf.ap(), whh_d, tag="wcast")
    mg0 = g.dma(fbr[0:64, :], g0b_d, tag="dg")
    g.dma(far[0:64, :], g0a_d, tag="dg")
    mg0 = ("dg_gpsimd", C.n["dg_gpsimd"])
    g.dma(mw0f[:], mw0_d, tag="dw")
    g.dma(mw1f[:].rearrange("p (k n) -> p k n", k=2),
          mw1_d.rearrange("(k p) n -> p k n", p=128), tag="dw")
    y.dmaT(afm[:], af_d, tag="daf")
    g.dma(aec[:], aea_d, tag="dw")
    g.dma(sli_f[:], sli_d, tag="dw")
    for row, src in ((rowq, bqf_d), (rowk, bkf_d), (rowv, bvf_d), (rowvp, bvp_d),
                     (rowo, bof_d), (rowg, lng_d), (rowb, lnb_d)):
        g.dma(row[:], src.rearrange("(a n) -> a n", a=1), tag="dw")
    g.dma(rowA[:], bih_d.rearrange("(a n) -> a n", a=1), tag="dw")
    g.dma(rowB[:], bhh_d.rearrange("(a n) -> a n", a=1), tag="dw")
    mwr = ("dw_gpsimd", C.n["dw_gpsimd"])

    v.wait(*mg0)
    v.tensor_tensor(far[0:64, :], far[0:64, :], fbr[0:64, :], ALU.add)
    v.inc("ev")
    s.wait("ev", C.n["ev"])
    s.activation(out=fbr[0:64, :], in_=far[0:64, :], func=AF.Square)
    s.inc("sc")
    v.wait("sc", C.n["sc"])
    for h in range(16):
        v.reduce_sum(mskd[0:64, h:h + 1], fbr[0:64, h * 256:(h + 1) * 256], axis=AX.X)
    drain(C)
    v.tensor_copy(sqcol[0:64, :], mskd[0:64, :])
    v.inc("ev")
    t.wait("ev", C.n["ev"])
    t.matmul(P_r[0:1, 0:16], cm["onesc_b"][0:64, :], sqcol[0:64, :], start=True, stop=False)
    t.inc("mm")
    mg1 = g.dma(fcr[:, :], g1a_d, tag="dg")
    mg1 = g.dma(fbr[:, 2048:4096], g1b_d, tag="dg")
    v.wait(*mg1)
    v.wait("mm", C.n["mm"])
    v.tensor_tensor(fcr[:, :], fcr[:, :], fbr[:, 2048:4096], ALU.add)
    v.inc("ev")
    s.wait("ev", C.n["ev"])
    s.activation(out=fbr[:, 2048:4096], in_=fcr[:, :], func=AF.Square)
    s.inc("sc")
    v.wait("sc", C.n["sc"])
    for h in range(16):
        v.reduce_sum(mskd[:, h:h + 1], fbr[:, 2048 + h * 128: 2048 + (h + 1) * 128], axis=AX.X)
    drain(C)
    v.tensor_copy(sqcol[:, :], mskd[:, :])
    v.inc("ev")
    t.wait("ev", C.n["ev"])
    t.matmul(P_r[0:1, 0:16], cm["onesc_b"][:], sqcol[:, :], start=False, stop=True)
    t.inc("mm")
    v.wait("mm", C.n["mm"])
    v.tensor_copy(nrow[:], P_r[0:1, 0:16])
    v.inc("ev")
    s.wait("ev", C.n["ev"])
    s.activation(out=nrow[:], in_=nrow[:], func=AF.Sqrt)
    s.inc("sc")
    v.wait("sc", C.n["sc"])
    v.tensor_scalar(nrow[:], nrow[:], 1e-6, None, ALU.add)
    recip_rows(C, nrow[:], nrow[:], "dx")
    v.tensor_scalar(cfr[:], nrow[:], 10.0, None, ALU.mult)
    v.tensor_scalar(cfr[:], cfr[:], 1.0, None, ALU.min)
    v.inc("ev")
    mae = g.dma(aec2[:], aeb_d, tag="dg")
    v.wait(*mae)
    v.wait(*mwr)
    v.tensor_tensor(aec[:], aec[:], aec2[:], ALU.add)
    drain(C)
    v.tensor_copy(aecb[:], aec[0:16, :])
    v.inc("ev")
    t.wait("ev", C.n["ev"])
    t.wait("dcw_sync", C.n["dcw_sync"])
    t.matmul(P_r[0:1, 16:32], aecb[:], i16[:], start=True, stop=True)
    t.inc("mm")
    v.wait("mm", C.n["mm"])
    v.tensor_scalar(s0r[:], P_r[0:1, 16:32], -1.0 / 8192.0, 1.0, ALU.mult, ALU.add)
    v.inc("ev")
    t.wait("ev", C.n["ev"])
    t.matmul(P_r[:, 32:48], cm["ones1_f"][:], s0r[:], start=True, stop=True)
    t.matmul(P_r[:, 48:64], cm["ones1_f"][:], cfr[:], start=True, stop=True)
    t.inc("mm")
    v.wait("mm", C.n["mm"])
    v.tensor_copy(s0c[:], P_r[:, 32:48])
    v.tensor_copy(cfc[:], P_r[:, 48:64])
    drain(C)
    v.tensor_tensor(fbr[0:64, :].rearrange("p (a b) -> p a b", a=16),
                    mw0f[:, None, :].broadcast_to([64, 16, 256]),
                    s0c[0:64, :, None].broadcast_to([64, 16, 256]), ALU.mult)
    v.tensor_tensor(far[0:64, :].rearrange("p (a b) -> p a b", a=16),
                    far[0:64, :].rearrange("p (a b) -> p a b", a=16),
                    cfc[0:64, :, None].broadcast_to([64, 16, 256]), ALU.mult)
    v.tensor_tensor(fbr[0:64, :], fbr[0:64, :], far[0:64, :], ALU.subtract)
    drain(C)
    v.tensor_copy(nW0b[0:64, :], fbr[0:64, :])
    drain(C)
    v.tensor_copy(nW0b[64:128, :], nW0b[0:64, :])
    v.inc("w0e")
    v.tensor_tensor(fbr[:, 0:2048].rearrange("p (a b) -> p a b", a=16),
                    mw1f[:, None, :].broadcast_to([128, 16, 128]),
                    s0c[:, :, None].broadcast_to([128, 16, 128]), ALU.mult)
    v.tensor_tensor(fcr[:, :].rearrange("p (a b) -> p a b", a=16),
                    fcr[:, :].rearrange("p (a b) -> p a b", a=16),
                    cfc[:, :, None].broadcast_to([128, 16, 128]), ALU.mult)
    v.tensor_tensor(fbr[:, 0:2048], fbr[:, 0:2048], fcr[:, :], ALU.subtract)
    drain(C)
    v.tensor_copy(nW1b[:], fbr[:, 0:2048])
    v.inc("w1e")
    C.barrier()

    msl = g.dma(gi_t[0:64, 0:1025], sla_d, tag="ds")
    msl = g.dma(gh_t[0:64, 0:1025], slb_d, tag="ds")
    v.wait(*msl)
    v.tensor_tensor(gi_t[0:64, 0:1025], gi_t[0:64, 0:1025], gh_t[0:64, 0:1025], ALU.add)
    v.tensor_scalar(den_c[:], gi_t[0:64, 1024:1025], 1e-8, None, ALU.add)
    recip_rows(C, den_c[:], den_c[:], "dx")
    v.tensor_scalar(si_t[:], gi_t[0:64, 0:1024], den_c[:, 0:1], None, ALU.mult)
    drain(C)
    v.tensor_copy(si_b[:], si_t[:])
    v.inc("ev")
    s.wait(*mwr)
    s.activation(out=Sn_b[:], in_=sli_f[:], func=AF.Identity)
    s.inc("sc")
    t.wait("ev", C.n["ev"])
    t.wait("sc", C.n["sc"])
    for c8 in range(8):
        t.wait("tre", C.n["tr"])
        t.transpose(P_cb[:, 0:64], si_b[:, c8 * 128:(c8 + 1) * 128], cm["ident_b"][0:64, 0:64])
        t.inc("tr")
        v.wait("tr", C.n["tr"])
        v.tensor_copy(siF[:, c8 * 64:(c8 + 1) * 64], P_cb[:, 0:64])
        v.inc("tre")
        t.wait("tre", C.n["tr"])
        t.transpose(P_cb[:, 0:64], Sn_b[:, c8 * 128:(c8 + 1) * 128], cm["ident_b"][0:64, 0:64])
        t.inc("tr")
        v.wait("tr", C.n["tr"])
        v.tensor_copy(sliF[:, c8 * 64:(c8 + 1) * 64], P_cb[:, 0:64])
        v.inc("tre")
    for which, xF, dst, brow in (("wih", siF, gi_t, rowA), ("whh", sliF, gh_t, rowB)):
        src_bf = wih_bf if which == "wih" else whh_bf
        wm = None
        for c8 in range(8):
            wm = y.dmaT(arena[:, c8 * 3072:(c8 + 1) * 3072],
                        src_bf.ap()[:, c8 * 128:(c8 + 1) * 128], tag="wT")
        t.wait(*wm)
        t.wait("tre", C.n["tre"])
        for oc in range(6):
            for k in range(8):
                t.matmul(P_a[0:64, :], xF[:, k * 64:(k + 1) * 64],
                         arena[:, k * 3072 + oc * 512:(k * 3072) + oc * 512 + 512],
                         start=(k == 0), stop=False)
            t.matmul(P_a[0:64, :], cm["ones1_b"][0:1, 0:64],
                     brow[:, oc * 512:(oc + 1) * 512], start=False, stop=True)
            t.inc("mm")
            v.wait("mm", C.n["mm"])
            v.tensor_copy(dst[0:64, oc * 512:(oc + 1) * 512], P_a[0:64, :])
            v.inc("gie")
            t.wait("gie", C.n["gie"])
        if which == "wih":
            C.barrier()
    C.barrier()
    v.tensor_tensor(gr_t[:], gi_t[0:64, 0:1024], gh_t[0:64, 0:1024], ALU.add)
    v.inc("ev")
    s.wait("ev", C.n["ev"])
    s.activation(out=gr_t[:], in_=gr_t[:], func=AF.Sigmoid)
    s.inc("sc")
    v.tensor_tensor(gn_t[:], gi_t[0:64, 1024:2048], gh_t[0:64, 1024:2048], ALU.add)
    v.inc("ev")
    s.wait("ev", C.n["ev"])
    s.activation(out=gn_t[:], in_=gn_t[:], func=AF.Sigmoid)
    s.inc("sc")
    v.wait("sc", C.n["sc"])
    v.tensor_tensor(Sn_t[:], gr_t[:], gh_t[0:64, 2048:3072], ALU.mult)
    v.tensor_tensor(Sn_t[:], Sn_t[:], gi_t[0:64, 2048:3072], ALU.add)
    v.inc("ev")
    s.wait("ev", C.n["ev"])
    s.activation(out=Sn_t[:], in_=Sn_t[:], func=AF.Tanh)
    s.inc("sc")
    v.wait("sc", C.n["sc"])
    v.tensor_tensor(gr_t[:], gn_t[:], Sn_t[:], ALU.mult)
    v.tensor_tensor(Sn_t[:], Sn_t[:], gr_t[:], ALU.subtract)
    v.tensor_tensor(gr_t[:], gn_t[:], sli_f[:], ALU.mult)
    v.tensor_tensor(Sn_t[:], Sn_t[:], gr_t[:], ALU.add)
    drain(C)
    v.tensor_copy(Sn_b[:], Sn_t[:])
    v.inc("ev")
    t.wait("ev", C.n["ev"])
    for c8 in range(8):
        t.wait("tre", C.n["tr"])
        t.transpose(P_cb[:, 0:64], Sn_b[:, c8 * 128:(c8 + 1) * 128], cm["ident_b"][0:64, 0:64])
        t.inc("tr")
        v.wait("tr", C.n["tr"])
        v.tensor_copy(SnF[:, c8 * 64:(c8 + 1) * 64], P_cb[:, 0:64])
        v.inc("tre")
    mqk = g.dma(wqk[:].rearrange("p (k n) -> p k n", k=8).rearrange("p k (u n) -> p k u n", u=2)
                if False else wqk[:, 0:2048].rearrange("p (k n) -> p k n", k=8),
                wqf_d.rearrange("(k p) n -> p k n", p=128), tag="dw2")
    mqk = g.dma(wqk[:, 2048:4096].rearrange("p (k n) -> p k n", k=8),
                wkf_d.rearrange("(k p) n -> p k n", p=128), tag="dw2")
    mv = g.dma(arena[:, 0:8192].rearrange("p (k n) -> p k n", k=8),
               wvf_d.rearrange("(k p) n -> p k n", p=128), tag="dw2")
    t.wait(*mqk)
    t.wait("tre", C.n["tre"])
    for which, wofs, brow, dstb in (("q", 0, rowq, qs_b), ("k", 2048, rowk, ks_b)):
        for k in range(8):
            t.matmul(P_b[0:64, 0:256], SnF[:, k * 64:(k + 1) * 64],
                     wqk[:, wofs + k * 256: wofs + (k + 1) * 256], start=(k == 0), stop=False)
        t.matmul(P_b[0:64, 0:256], cm["ones1_f"][0:1, 0:64], brow[:], start=False, stop=True)
        t.inc("mm")
        v.wait("mm", C.n["mm"])
        v.tensor_copy(dstb[:], P_b[0:64, 0:256])
        v.inc("qke")
        t.wait("qke", C.n["qke"])
    t.wait(*mv)
    for half in range(2):
        for k in range(8):
            t.matmul(P_c[0:64, :], SnF[:, k * 64:(k + 1) * 64],
                     arena[:, k * 1024 + half * 512: k * 1024 + half * 512 + 512],
                     start=(k == 0), stop=False)
        t.matmul(P_c[0:64, :], cm["ones1_f"][0:1, 0:64],
                 rowv[:, half * 512:(half + 1) * 512], start=False, stop=True)
        t.inc("mm")
        v.wait("mm", C.n["mm"])
        v.tensor_copy(vs_t[:, half * 512:(half + 1) * 512], P_c[0:64, :])
        v.inc("vse")
        t.wait("vse", C.n["vse"])
    for src, dstF in ((qs_b, qsF), (ks_b, ksF)):
        for c2 in range(2):
            t.wait("tre", C.n["tr"])
            t.transpose(P_cb[:, 0:64], src[:, c2 * 128:(c2 + 1) * 128], cm["ident_b"][0:64, 0:64])
            t.inc("tr")
            v.wait("tr", C.n["tr"])
            v.tensor_copy(dstF[:, c2 * 64:(c2 + 1) * 64], P_cb[:, 0:64])
            v.inc("tre")
    t.wait("tre", C.n["tre"])
    for c2 in range(2):
        t.matmul(P_a[0:64, 0:64], qsF[:, c2 * 64:(c2 + 1) * 64], ksF[:, c2 * 64:(c2 + 1) * 64],
                 start=(c2 == 0), stop=(c2 == 1))
    t.inc("mm")
    v.wait("mm", C.n["mm"])
    v.tensor_scalar(att_t[:], P_a[0:64, 0:64], 1.0 / 16.0, None, ALU.mult)
    drain(C)
    v.reduce_max(rmx_c[:], att_t[:], axis=AX.X)
    drain(C)
    v.tensor_scalar(rmx_c[:], rmx_c[:], -1.0, None, ALU.mult)
    v.inc("ev")
    s.wait("ev", C.n["ev"])
    s.activation(out=att_t[:], in_=att_t[:], func=AF.Exp, bias=rmx_c[:, 0:1])
    s.inc("sc")
    v.wait("sc", C.n["sc"])
    v.tensor_copy(C.dmb[:], C.dmb[:])
    v.reduce_sum(ssm_c[:], att_t[:], axis=AX.X)
    recip_rows(C, ssm_c[:], ssm_c[:], "dx")
    v.tensor_scalar(attb[:], att_t[:], ssm_c[:, 0:1], None, ALU.mult)
    drain(C)
    t.wait("drv", C.n["drv"])
    t.wait("tre", C.n["tr"])
    t.transpose(P_cb[0:64, 0:64], attb[:], cm["ident_b"][0:64, 0:64])
    t.inc("tr")
    v.wait("tr", C.n["tr"])
    v.tensor_copy(attT[:], P_cb[0:64, 0:64])
    v.inc("tre")
    drain(C)
    v.tensor_copy(si_b[:], vs_t[:])
    v.inc("ev")
    t.wait("ev", C.n["ev"])
    t.wait("tre", C.n["tre"])
    for half in range(2):
        t.matmul(P_a[0:64, :], attT[:], si_b[:, half * 512:(half + 1) * 512],
                 start=True, stop=True)
        t.inc("mm")
        v.wait("mm", C.n["mm"])
        v.tensor_tensor(y_t[:, half * 512:(half + 1) * 512], Sn_t[:, half * 512:(half + 1) * 512],
                        P_a[0:64, :], ALU.add)
        v.inc("ye")
        t.wait("ye", C.n["ye"])
    v.reduce_sum(mu_c[:], y_t[:], axis=AX.X)
    drain(C)
    v.tensor_scalar(mu_c[:], mu_c[:], 1.0 / 1024.0, None, ALU.mult)
    v.inc("ev")
    s.wait("ev", C.n["ev"])
    s.activation(out=gr_t[:], in_=y_t[:], func=AF.Square)
    s.inc("sc")
    v.wait("sc", C.n["sc"])
    v.reduce_sum(var_c[:], gr_t[:], axis=AX.X)
    drain(C)
    v.tensor_scalar(var_c[:], var_c[:], 1.0 / 1024.0, None, ALU.mult)
    v.tensor_tensor(gn_t[0:64, 0:1], mu_c[:], mu_c[:], ALU.mult)
    v.tensor_tensor(var_c[:], var_c[:], gn_t[0:64, 0:1], ALU.subtract)
    v.tensor_scalar(var_c[:], var_c[:], 1e-5, None, ALU.add)
    v.inc("ev")
    s.wait("ev", C.n["ev"])
    s.activation(out=var_c[:], in_=var_c[:], func=AF.Sqrt)
    s.inc("sc")
    v.wait("sc", C.n["sc"])
    recip_rows(C, var_c[:], var_c[:], "dx")
    t.wait("drv", C.n["drv"])
    for half in range(2):
        t.matmul(P_b[0:64, :], cm["ones1_f"][0:1, 0:64], rowg[:, half * 512:(half + 1) * 512],
                 start=True, stop=True)
        t.matmul(P_c[0:64, :], cm["ones1_f"][0:1, 0:64], rowb[:, half * 512:(half + 1) * 512],
                 start=True, stop=True)
        t.inc("mm")
        v.wait("mm", C.n["mm"])
        v.tensor_copy(lngb[:, half * 512:(half + 1) * 512], P_b[0:64, :])
        v.tensor_copy(lnbb[:, half * 512:(half + 1) * 512], P_c[0:64, :])
        v.inc("lge")
        t.wait("lge", C.n["lge"])
    v.tensor_scalar(y_t[:], y_t[:], mu_c[:, 0:1], None, ALU.subtract)
    v.tensor_scalar(y_t[:], y_t[:], var_c[:, 0:1], None, ALU.mult)
    v.tensor_tensor(y_t[:], y_t[:], lngb[:], ALU.mult)
    v.tensor_tensor(y_t[:], y_t[:], lnbb[:], ALU.add)
    drain(C)
    v.tensor_copy(si_b[:], y_t[:])
    v.inc("ev")
    t.wait("ev", C.n["ev"])
    for c8 in range(8):
        t.wait("tre", C.n["tr"])
        t.transpose(P_cb[:, 0:64], si_b[:, c8 * 128:(c8 + 1) * 128], cm["ident_b"][0:64, 0:64])
        t.inc("tr")
        v.wait("tr", C.n["tr"])
        v.tensor_copy(yF[:, c8 * 64:(c8 + 1) * 64], P_cb[:, 0:64])
        v.inc("tre")
    mvp = g.dma(arena[:, 8192:16384].rearrange("p (k n) -> p k n", k=8),
                wvp_d.rearrange("(k p) n -> p k n", p=128), tag="dw2")
    mwo = g.dma(arena[:, 16384:24576].rearrange("p (k n) -> p k n", k=8),
                wof_d.rearrange("(k p) n -> p k n", p=128), tag="dw2")
    t.wait(*mvp)
    t.wait("tre", C.n["tre"])
    for half in range(2):
        for k in range(8):
            t.matmul(P_a[0:64, :], yF[:, k * 64:(k + 1) * 64],
                     arena[:, 8192 + k * 1024 + half * 512: 8192 + k * 1024 + half * 512 + 512],
                     start=(k == 0), stop=False)
        t.matmul(P_a[0:64, :], cm["ones1_f"][0:1, 0:64],
                 rowvp[:, half * 512:(half + 1) * 512], start=False, stop=True)
        t.inc("mm")
        v.wait("mm", C.n["mm"])
        v.tensor_copy(sv_t[:, half * 512:(half + 1) * 512], P_a[0:64, :])
        v.inc("sve")
        t.wait("sve", C.n["sve"])
    drain(C)
    v.tensor_copy(si_b[:], sv_t[:])
    v.inc("ev")
    t.wait("ev", C.n["ev"])
    for c8 in range(8):
        t.wait("tre", C.n["tr"])
        t.transpose(P_cb[:, 0:64], si_b[:, c8 * 128:(c8 + 1) * 128], cm["ident_b"][0:64, 0:64])
        t.inc("tr")
        v.wait("tr", C.n["tr"])
        v.tensor_copy(svF[:, c8 * 64:(c8 + 1) * 64], P_cb[:, 0:64])
        v.inc("tre")
    t.wait(*mwo)
    t.wait("tre", C.n["tre"])
    for half in range(2):
        for k in range(8):
            t.matmul(P_b[0:64, :], svF[:, k * 64:(k + 1) * 64],
                     arena[:, 16384 + k * 1024 + half * 512: 16384 + k * 1024 + half * 512 + 512],
                     start=(k == 0), stop=(k == 7))
        t.inc("mm")
        v.wait("mm", C.n["mm"])
        v.tensor_copy(svwb[:, half * 512:(half + 1) * 512], P_b[0:64, :])
        v.inc("swe")
        t.wait("swe", C.n["swe"])
    for half in range(2):
        t.matmul(P_c[:, :], cm["ones1_f"][:], rowo[:, half * 512:(half + 1) * 512],
                 start=True, stop=True)
        t.inc("mm")
        v.wait("mm", C.n["mm"])
        v.tensor_copy(bofb[:, half * 512:(half + 1) * 512], P_c[:, :])
        v.inc("boe")
        t.wait("boe", C.n["boe"])
    C.barrier()

    for tg in range(8):
        lo = tg * TG
        mx1 = g.dma(qtg[:].rearrange("p (c u) -> p c u", c=8),
                    q_d.rearrange("(c p) u -> p c u", p=128)[:, :, lo:lo + TG], tag="dx")
        mx2 = g.dma(x1tg[:].rearrange("p (c u) -> p c u", c=8),
                    x1_d.rearrange("(c p) u -> p c u", p=128)[:, :, lo:lo + TG], tag="dx")
        t.wait(*mx2)
        v.wait(*mx2)
        s.wait(*mx2)
        for hp in range(8):
            hA, hB = 2 * hp, 2 * hp + 1
            for hl, h in ((0, hA), (1, hB)):
                rb_ = (h % 2) * 64
                ch_ = h // 2
                for mc in range(2):
                    ps = (P_a, P_b)[mc]
                    t.wait("fe", C.n["fe"])
                    t.matmul(ps[:], nW0b[rb_:rb_ + 64, h * 256 + mc * 128: h * 256 + (mc + 1) * 128],
                             qtg[rb_:rb_ + 64, ch_ * TG: (ch_ + 1) * TG], start=True, stop=True)
                    t.inc("mm")
                    s.wait("mm", C.n["mm"])
                    s.activation(out=act1rf[:, (hl * 2 + mc) * 512:(hl * 2 + mc + 1) * 512],
                                 in_=ps[:], func=AF.Gelu)
                    s.inc("sc")
                    C.ops["scalar"].append(("wait", C.semname("sc"), C.n["sc"]))
                    v.wait("sc", C.n["sc"])
                    v.tensor_copy(C.dmb[:], C.dmb[:])
                    v.inc("fe")
            t.wait("sc", C.n["sc"])
            for tc in range(4):
                for hl, h in ((0, hA), (1, hB)):
                    t.wait("pe", C.n["pe"])
                    for hc in range(2):
                        t.matmul(P_c[:, 0:64],
                                 act1rf[:, (hl * 2 + hc) * 512 + tc * 128: (hl * 2 + hc) * 512 + (tc + 1) * 128],
                                 nW1b[:, h * 128 + hc * 64: h * 128 + (hc + 1) * 64],
                                 start=(hc == 0), stop=(hc == 1))
                    t.inc("mm")
                    v.wait("mm", C.n["mm"])
                    v.tensor_copy(prer[:, tc * 1024 + h * 64: tc * 1024 + (h + 1) * 64], P_c[:, 0:64])
                    v.inc("pe")
        C.barrier()
        for tc in range(4):
            for c8 in range(8):
                t.wait("tre", C.n["tr"])
                t.transpose(P_cb[:], qtg[:, c8 * TG + tc * 128: c8 * TG + (tc + 1) * 128],
                            cm["ident_b"][:])
                t.inc("tr")
                v.wait("tr", C.n["tr"])
                v.tensor_copy(q_tm[:, c8 * 128:(c8 + 1) * 128], P_cb[:])
                v.inc("tre")
                t.wait("tre", C.n["tr"])
                t.transpose(P_cb[:], x1tg[:, c8 * TG + tc * 128: c8 * TG + (tc + 1) * 128],
                            cm["ident_b"][:])
                t.inc("tr")
                v.wait("tr", C.n["tr"])
                v.tensor_copy(x1_tm[:, c8 * 128:(c8 + 1) * 128], P_cb[:])
                v.inc("tre")
            s.wait("tre", C.n["tre"])
            s.activation(out=sqd[:], in_=prer[:, tc * 1024:(tc + 1) * 1024], func=AF.Square)
            s.inc("sc")
            v.wait("sc", C.n["sc"])
            for h16 in range(16):
                v.reduce_sum(mskd[:, h16:h16 + 1], sqd[:, h16 * 64:(h16 + 1) * 64], axis=AX.X)
            drain(C)
            v.tensor_scalar(mskd[:], mskd[:], 1.0 / 64.0, 1e-8, ALU.mult, ALU.add)
            v.inc("ev")
            s.wait("ev", C.n["ev"])
            s.activation(out=rrd[:], in_=mskd[:], func=AF.Sqrt)
            s.inc("sc")
            v.wait("sc", C.n["sc"])
            recip_rows(C, rrd[:], rrd[:], "dx")
            xa = x2acc[:, tc * 1024:(tc + 1) * 1024]
            v.tensor_tensor(xa.rearrange("p (a b) -> p a b", a=16),
                            prer[:, tc * 1024:(tc + 1) * 1024].rearrange("p (a b) -> p a b", a=16),
                            rrd[:, :, None].broadcast_to([128, 16, 64]), ALU.mult)
            v.tensor_tensor(xa, xa, q_tm[:], ALU.add)
            v.tensor_tensor(xa, xa, x1_tm[:], ALU.add)
            v.tensor_tensor(xa, xa, bofb[:], ALU.add)
            v.inc("ev")
            t.wait("ev", C.n["ev"])
            for half in range(2):
                ps = (P_a, P_b)[half]
                t.matmul(ps[:], afm[0:64, lo + tc * 128: lo + (tc + 1) * 128],
                         svwb[0:64, half * 512:(half + 1) * 512], start=True, stop=True)
                t.inc("f1m")
                v.wait("f1m", C.n["f1m"])
                v.tensor_tensor(xa[:, half * 512:(half + 1) * 512],
                                xa[:, half * 512:(half + 1) * 512], ps[:], ALU.add)
                v.inc("f1e")
            t.wait("f1e", C.n["f1e"])
            y.wait("f1e", C.n["f1e"])
            y.dma(out_o[lo + tc * 128: lo + (tc + 1) * 128, :], xa, tag="do")
        C.barrier()
    C.barrier()
    return C


def build_d2(race=True, C=None, io=None):
    if C is None:
        C = Ctx("d", race=race)
    T = 4096
    TG = 512
    if io is None:
        io = {"x1": C.din("x1", [1024, T], BF16),
              "q": C.din("q", [1024, T], BF16),
              "af": C.din("af", [T, 128], BF16),
              "g0a": C.din("g0a", [64, 4096], F32),
              "g0b": C.din("g0b", [64, 4096], F32),
              "g1a": C.din("g1a", [128, 2048], F32),
              "g1b": C.din("g1b", [128, 2048], F32),
              "aea": C.din("aea", [48, 1], F32),
              "aeb": C.din("aeb", [48, 1], F32),
              "sla": C.din("sla", [64, 1025], F32),
              "slb": C.din("slb", [64, 1025], F32),
              "mW0": C.din("mW0", [64, 256], F32),
              "mW1": C.din("mW1", [256, 64], F32),
              "slot_init": C.din("slot_init", [64, 1024], F32),
              "Wih": C.din("Wih", [3072, 1024], F32),
              "Whh": C.din("Whh", [3072, 1024], F32),
              "bih": C.din("bih", [3072], F32),
              "bhh": C.din("bhh", [3072], F32),
              "wq": C.din("wq", [1024, 256], F32),
              "bq": C.din("bq", [256], F32),
              "wk": C.din("wk", [1024, 256], F32),
              "bk": C.din("bk", [256], F32),
              "wv": C.din("wv", [1024, 1024], F32),
              "bv": C.din("bv", [1024], F32),
              "lng": C.din("lng", [1024], F32),
              "lnb": C.din("lnb", [1024], F32),
              "wvp": C.din("wvp", [1024, 1024], F32),
              "bvp": C.din("bvp", [1024], F32),
              "wo": C.din("wo", [1024, 1024], F32),
              "bo": C.din("bo", [1024], F32),
              "out_o": C.dout("out_o", [1024, T], F32)}
    x1_d = io["x1"]; q_d = io["q"]; af_d = io["af"]
    g0a_d = io["g0a"]; g0b_d = io["g0b"]; g1a_d = io["g1a"]; g1b_d = io["g1b"]
    aea_d = io["aea"]; aeb_d = io["aeb"]; sla_d = io["sla"]; slb_d = io["slb"]
    mw0_d = io["mW0"]; mw1_d = io["mW1"]; sli_d = io["slot_init"]
    wih_d = io["Wih"]; whh_d = io["Whh"]; bih_d = io["bih"]; bhh_d = io["bhh"]
    wqf_d = io["wq"]; bqf_d = io["bq"]; wkf_d = io["wk"]; bkf_d = io["bk"]
    wvf_d = io["wv"]; bvf_d = io["bv"]; lng_d = io["lng"]; lnb_d = io["lnb"]
    wvp_d = io["wvp"]; bvp_d = io["bvp"]; wof_d = io["wo"]; bof_d = io["bo"]
    out_o = io["out_o"]
    wih_bf = C.dint(C.prefix + "wih_bf", [3072, 1024], BF16)
    whh_bf = C.dint(C.prefix + "whh_bf", [3072, 1024], BF16)

    arena = C.sbuf("d_arena", [128, 24576], BF16)
    far = C.sbuf("d_far", [128, 4096], F32)
    fbr = C.sbuf("d_fbr", [128, 4096], F32)
    fcr = C.sbuf("d_fcr", [128, 2048], F32)
    nW0b = C.sbuf("d_nW0b", [128, 4096], BF16)
    nW1b = C.sbuf("d_nW1b", [128, 2048], BF16)
    afm = C.sbuf("d_afm", [128, 4096], BF16)
    svwb = C.sbuf("d_svwb", [64, 1024], BF16)
    bofc = C.sbuf("d_bofc", [128, 8], F32)
    sel16d = C.sbuf("d_sel16", [128, 128], BF16)
    expdd = C.sbuf("d_expd", [16, 1024], BF16)
    rqfd = C.sbuf("d_rqfd", [16, 512], F32)
    rq16d = C.sbuf("d_rq16", [16, 512], BF16)
    qtg = arena[:, 0:4096]
    x1tg = arena[:, 4096:8192]
    act1rf = arena[:, 8192:10240]
    prer = arena[:, 10240:14336]
    sqf = arena[:, 14336:18432]
    mskd = C.sbuf("d_mskd", [128, 16], F32)
    rrd = C.sbuf("d_rrd", [128, 16], F32)
    wqk = C.sbuf("d_wqk", [128, 4096], BF16)
    sli_f = C.sbuf("d_slif", [64, 1024], F32)
    sliF = C.sbuf("d_sliF", [128, 512], BF16)
    si_b = C.sbuf("d_sib", [64, 1024], BF16)
    siF = C.sbuf("d_siF", [128, 512], BF16)
    gi_t = far[0:64, 0:3072]
    gn_t = far[0:64, 3072:4096]
    gh_t = fbr[0:64, 0:3072]
    Sn_t = fbr[0:64, 3072:4096]
    si_t = fcr[0:64, 0:1024]
    gr_t = fcr[0:64, 1024:2048]
    vs_t = far[0:64, 0:1024]
    y_t = far[0:64, 1024:2048]
    sv_t = far[0:64, 2048:3072]
    x2acc = far[:, 0:4096]
    Sn_b = C.sbuf("d_Snb", [64, 1024], BF16)
    SnF = C.sbuf("d_SnF", [128, 512], BF16)
    qs_b = C.sbuf("d_qsb", [64, 256], BF16)
    ks_b = C.sbuf("d_ksb", [64, 256], BF16)
    qsF = C.sbuf("d_qsF", [128, 128], BF16)
    ksF = C.sbuf("d_ksF", [128, 128], BF16)
    att_t = C.sbuf("d_attt", [64, 64], F32)
    attb = C.sbuf("d_attb", [64, 64], BF16)
    attT = C.sbuf("d_attT", [64, 64], BF16)
    yF = C.sbuf("d_yF", [128, 512], BF16)
    svF = C.sbuf("d_svF", [128, 512], BF16)
    lngb = C.sbuf("d_lngb", [64, 1024], F32)
    lnbb = C.sbuf("d_lnbb", [64, 1024], F32)
    rowA = C.sbuf("d_rowA", [1, 3072], BF16)
    rowB = C.sbuf("d_rowB", [1, 3072], BF16)
    rowq = C.sbuf("d_rowq", [1, 256], F32)
    rowk = C.sbuf("d_rowk", [1, 256], F32)
    rowv = C.sbuf("d_rowv", [1, 1024], F32)
    rowvp = C.sbuf("d_rowvp", [1, 1024], F32)
    rowo = C.sbuf("d_rowo", [1, 1024], F32)
    rowg = C.sbuf("d_rowg", [1, 1024], F32)
    rowb = C.sbuf("d_rowb", [1, 1024], F32)
    aec = C.sbuf("d_aec", [48, 1], F32)
    aecb = C.sbuf("d_aecb", [16, 1], BF16)
    aec2 = C.sbuf("d_aec2", [48, 1], F32)
    sqcol = C.sbuf("d_sqcol", [128, 16], BF16)
    nrow = C.sbuf("d_nrow", [1, 16], F32)
    amr = C.sbuf("d_amr", [1, 16], F32)
    s0r = C.sbuf("d_s0r", [1, 16], F32)
    cfr = C.sbuf("d_cfr", [1, 16], F32)
    s0c = C.sbuf("d_s0c", [128, 16], F32)
    cfc = C.sbuf("d_cfc", [128, 16], F32)
    mw0f = C.sbuf("d_mw0f", [64, 256], F32)
    mw1f = C.sbuf("d_mw1f", [128, 128], F32)
    mu_c = C.sbuf("d_muc", [64, 1], F32)
    var_c = C.sbuf("d_varc", [64, 1], F32)
    rmx_c = C.sbuf("d_rmxc", [64, 1], F32)
    ssm_c = C.sbuf("d_ssmc", [64, 1], F32)
    den_c = C.sbuf("d_denc", [64, 1], F32)

    P_a = C.psum("P_a", [128, 512])
    P_b = C.psum("P_b", [128, 512])
    P_c = C.psum("P_c", [128, 512])
    P_cb = C.psum("P_cb", [128, 128], BF16)
    P_r = C.psum("P_r", [128, 512])

    cm = C.std_consts()
    i16 = C.sbuf("d_i16", [16, 16], BF16)
    t, v, s, y, g = C.t, C.v, C.s, C.y, C.g
    y.dma(i16[:], C.const("k_ident16b", np.eye(16, dtype=ml_dtypes.bfloat16)), tag="dcw")
    for nm in ("mm", "ev", "sc", "tr", "tre", "dx", "drv", "w0e", "w1e", "gie",
               "qke", "vse", "ye", "lge", "sve", "swe", "boe", "fe", "pe",
               "f1m", "f1e"):
        C.n.setdefault(nm, 0)

    g.dma(wih_bf.ap(), wih_d, tag="wcast")
    g.dma(whh_bf.ap(), whh_d, tag="wcast")
    mg0 = g.dma(fbr[0:64, :], g0b_d, tag="dg")
    g.dma(far[0:64, :], g0a_d, tag="dg")
    mg0 = ("dg_gpsimd", C.n["dg_gpsimd"])
    g.dma(mw0f[:], mw0_d, tag="dw")
    g.dma(mw1f[:].rearrange("p (k n) -> p k n", k=2),
          mw1_d.rearrange("(k p) n -> p k n", p=128), tag="dw")
    y.dmaT(afm[:], af_d, tag="daf")
    g.dma(aec[:], aea_d, tag="dw")
    g.dma(sli_f[:], sli_d, tag="dw")
    for row, src in ((rowq, bqf_d), (rowk, bkf_d), (rowv, bvf_d), (rowvp, bvp_d),
                     (rowo, bof_d), (rowg, lng_d), (rowb, lnb_d)):
        g.dma(row[:], src.rearrange("(a n) -> a n", a=1), tag="dw")
    g.dma(rowA[:], bih_d.rearrange("(a n) -> a n", a=1), tag="dw")
    g.dma(rowB[:], bhh_d.rearrange("(a n) -> a n", a=1), tag="dw")
    mwr = ("dw_gpsimd", C.n["dw_gpsimd"])

    v.wait(*mg0)
    v.tensor_tensor(far[0:64, :], far[0:64, :], fbr[0:64, :], ALU.add)
    v.inc("ev")
    s.wait("ev", C.n["ev"])
    s.activation(out=fbr[0:64, :], in_=far[0:64, :], func=AF.Square)
    s.inc("sc")
    v.wait("sc", C.n["sc"])
    for h in range(16):
        v.reduce_sum(mskd[0:64, h:h + 1], fbr[0:64, h * 256:(h + 1) * 256], axis=AX.X)
    drain(C)
    v.tensor_copy(sqcol[0:64, :], mskd[0:64, :])
    v.inc("ev")
    t.wait("ev", C.n["ev"])
    t.matmul(P_r[0:1, 0:16], cm["onesc_b"][0:64, :], sqcol[0:64, :], start=True, stop=False)
    t.inc("mm")
    mg1 = g.dma(fcr[:, :], g1a_d, tag="dg")
    mg1 = g.dma(fbr[:, 2048:4096], g1b_d, tag="dg")
    v.wait(*mg1)
    v.wait("mm", C.n["mm"])
    v.tensor_tensor(fcr[:, :], fcr[:, :], fbr[:, 2048:4096], ALU.add)
    v.inc("ev")
    s.wait("ev", C.n["ev"])
    s.activation(out=fbr[:, 2048:4096], in_=fcr[:, :], func=AF.Square)
    s.inc("sc")
    v.wait("sc", C.n["sc"])
    for h in range(16):
        v.reduce_sum(mskd[:, h:h + 1], fbr[:, 2048 + h * 128: 2048 + (h + 1) * 128], axis=AX.X)
    drain(C)
    v.tensor_copy(sqcol[:, :], mskd[:, :])
    v.inc("ev")
    t.wait("ev", C.n["ev"])
    t.matmul(P_r[0:1, 0:16], cm["onesc_b"][:], sqcol[:, :], start=False, stop=True)
    t.inc("mm")
    v.wait("mm", C.n["mm"])
    v.tensor_copy(nrow[:], P_r[0:1, 0:16])
    v.inc("ev")
    s.wait("ev", C.n["ev"])
    s.activation(out=nrow[:], in_=nrow[:], func=AF.Sqrt)
    s.inc("sc")
    v.wait("sc", C.n["sc"])
    v.tensor_scalar(nrow[:], nrow[:], 1e-6, None, ALU.add)
    recip_rows(C, nrow[:], nrow[:], "dx")
    v.tensor_scalar(cfr[:], nrow[:], 10.0, None, ALU.mult)
    v.tensor_scalar(cfr[:], cfr[:], 1.0, None, ALU.min)
    v.inc("ev")
    mae = g.dma(aec2[:], aeb_d, tag="dg")
    v.wait(*mae)
    v.wait(*mwr)
    v.tensor_tensor(aec[:], aec[:], aec2[:], ALU.add)
    drain(C)
    v.tensor_copy(aecb[:], aec[0:16, :])
    v.inc("ev")
    t.wait("ev", C.n["ev"])
    t.wait("dcw_sync", C.n["dcw_sync"])
    t.matmul(P_r[0:1, 16:32], aecb[:], i16[:], start=True, stop=True)
    t.inc("mm")
    v.wait("mm", C.n["mm"])
    v.tensor_scalar(s0r[:], P_r[0:1, 16:32], -1.0 / 8192.0, 1.0, ALU.mult, ALU.add)
    v.inc("ev")
    t.wait("ev", C.n["ev"])
    t.matmul(P_r[:, 32:48], cm["ones1_f"][:], s0r[:], start=True, stop=True)
    t.matmul(P_r[:, 48:64], cm["ones1_f"][:], cfr[:], start=True, stop=True)
    t.inc("mm")
    v.wait("mm", C.n["mm"])
    v.tensor_copy(s0c[:], P_r[:, 32:48])
    v.tensor_copy(cfc[:], P_r[:, 48:64])
    drain(C)
    v.tensor_tensor(fbr[0:64, :].rearrange("p (a b) -> p a b", a=16),
                    mw0f[:, None, :].broadcast_to([64, 16, 256]),
                    s0c[0:64, :, None].broadcast_to([64, 16, 256]), ALU.mult)
    v.tensor_tensor(far[0:64, :].rearrange("p (a b) -> p a b", a=16),
                    far[0:64, :].rearrange("p (a b) -> p a b", a=16),
                    cfc[0:64, :, None].broadcast_to([64, 16, 256]), ALU.mult)
    v.tensor_tensor(fbr[0:64, :], fbr[0:64, :], far[0:64, :], ALU.subtract)
    drain(C)
    v.tensor_copy(nW0b[0:64, :], fbr[0:64, :])
    drain(C)
    v.tensor_copy(nW0b[64:128, :], nW0b[0:64, :])
    v.inc("w0e")
    v.tensor_tensor(fbr[:, 0:2048].rearrange("p (a b) -> p a b", a=16),
                    mw1f[:, None, :].broadcast_to([128, 16, 128]),
                    s0c[:, :, None].broadcast_to([128, 16, 128]), ALU.mult)
    v.tensor_tensor(fcr[:, :].rearrange("p (a b) -> p a b", a=16),
                    fcr[:, :].rearrange("p (a b) -> p a b", a=16),
                    cfc[:, :, None].broadcast_to([128, 16, 128]), ALU.mult)
    v.tensor_tensor(fbr[:, 0:2048], fbr[:, 0:2048], fcr[:, :], ALU.subtract)
    drain(C)
    v.tensor_copy(nW1b[:], fbr[:, 0:2048])
    v.inc("w1e")
    C.barrier()

    msl = g.dma(gi_t[0:64, 0:1025], sla_d, tag="ds")
    msl = g.dma(gh_t[0:64, 0:1025], slb_d, tag="ds")
    v.wait(*msl)
    v.tensor_tensor(gi_t[0:64, 0:1025], gi_t[0:64, 0:1025], gh_t[0:64, 0:1025], ALU.add)
    v.tensor_scalar(den_c[:], gi_t[0:64, 1024:1025], 1e-8, None, ALU.add)
    recip_rows(C, den_c[:], den_c[:], "dx")
    v.tensor_scalar(si_t[:], gi_t[0:64, 0:1024], den_c[:, 0:1], None, ALU.mult)
    drain(C)
    v.tensor_copy(si_b[:], si_t[:])
    v.inc("ev")
    s.wait(*mwr)
    s.activation(out=Sn_b[:], in_=sli_f[:], func=AF.Identity)
    s.inc("sc")
    t.wait("ev", C.n["ev"])
    t.wait("sc", C.n["sc"])
    for c8 in range(8):
        t.wait("tre", C.n["tr"])
        t.transpose(P_cb[:, 0:64], si_b[:, c8 * 128:(c8 + 1) * 128], cm["ident_b"][0:64, 0:64])
        t.inc("tr")
        v.wait("tr", C.n["tr"])
        v.tensor_copy(siF[:, c8 * 64:(c8 + 1) * 64], P_cb[:, 0:64])
        v.inc("tre")
        t.wait("tre", C.n["tr"])
        t.transpose(P_cb[:, 0:64], Sn_b[:, c8 * 128:(c8 + 1) * 128], cm["ident_b"][0:64, 0:64])
        t.inc("tr")
        v.wait("tr", C.n["tr"])
        v.tensor_copy(sliF[:, c8 * 64:(c8 + 1) * 64], P_cb[:, 0:64])
        v.inc("tre")
    for which, xF, dst, brow in (("wih", siF, gi_t, rowA), ("whh", sliF, gh_t, rowB)):
        src_bf = wih_bf if which == "wih" else whh_bf
        wm = None
        for c8 in range(8):
            wm = y.dmaT(arena[:, c8 * 3072:(c8 + 1) * 3072],
                        src_bf.ap()[:, c8 * 128:(c8 + 1) * 128], tag="wT")
        t.wait(*wm)
        t.wait("tre", C.n["tre"])
        for oc in range(6):
            for k in range(8):
                t.matmul(P_a[0:64, :], xF[:, k * 64:(k + 1) * 64],
                         arena[:, k * 3072 + oc * 512:(k * 3072) + oc * 512 + 512],
                         start=(k == 0), stop=False)
            t.matmul(P_a[0:64, :], cm["ones1_b"][0:1, 0:64],
                     brow[:, oc * 512:(oc + 1) * 512], start=False, stop=True)
            t.inc("mm")
            v.wait("mm", C.n["mm"])
            v.tensor_copy(dst[0:64, oc * 512:(oc + 1) * 512], P_a[0:64, :])
            v.inc("gie")
            t.wait("gie", C.n["gie"])
        if which == "wih":
            C.barrier()
    C.barrier()
    v.tensor_tensor(gr_t[:], gi_t[0:64, 0:1024], gh_t[0:64, 0:1024], ALU.add)
    v.inc("ev")
    s.wait("ev", C.n["ev"])
    s.activation(out=gr_t[:], in_=gr_t[:], func=AF.Sigmoid)
    s.inc("sc")
    v.tensor_tensor(gn_t[:], gi_t[0:64, 1024:2048], gh_t[0:64, 1024:2048], ALU.add)
    v.inc("ev")
    s.wait("ev", C.n["ev"])
    s.activation(out=gn_t[:], in_=gn_t[:], func=AF.Sigmoid)
    s.inc("sc")
    v.wait("sc", C.n["sc"])
    v.tensor_tensor(Sn_t[:], gr_t[:], gh_t[0:64, 2048:3072], ALU.mult)
    v.tensor_tensor(Sn_t[:], Sn_t[:], gi_t[0:64, 2048:3072], ALU.add)
    v.inc("ev")
    s.wait("ev", C.n["ev"])
    s.activation(out=Sn_t[:], in_=Sn_t[:], func=AF.Tanh)
    s.inc("sc")
    v.wait("sc", C.n["sc"])
    v.tensor_tensor(gr_t[:], gn_t[:], Sn_t[:], ALU.mult)
    v.tensor_tensor(Sn_t[:], Sn_t[:], gr_t[:], ALU.subtract)
    v.tensor_tensor(gr_t[:], gn_t[:], sli_f[:], ALU.mult)
    v.tensor_tensor(Sn_t[:], Sn_t[:], gr_t[:], ALU.add)
    drain(C)
    v.tensor_copy(Sn_b[:], Sn_t[:])
    v.inc("ev")
    t.wait("ev", C.n["ev"])
    for c8 in range(8):
        t.wait("tre", C.n["tr"])
        t.transpose(P_cb[:, 0:64], Sn_b[:, c8 * 128:(c8 + 1) * 128], cm["ident_b"][0:64, 0:64])
        t.inc("tr")
        v.wait("tr", C.n["tr"])
        v.tensor_copy(SnF[:, c8 * 64:(c8 + 1) * 64], P_cb[:, 0:64])
        v.inc("tre")
    mqk = g.dma(wqk[:].rearrange("p (k n) -> p k n", k=8).rearrange("p k (u n) -> p k u n", u=2)
                if False else wqk[:, 0:2048].rearrange("p (k n) -> p k n", k=8),
                wqf_d.rearrange("(k p) n -> p k n", p=128), tag="dw2")
    mqk = g.dma(wqk[:, 2048:4096].rearrange("p (k n) -> p k n", k=8),
                wkf_d.rearrange("(k p) n -> p k n", p=128), tag="dw2")
    mv = g.dma(arena[:, 0:8192].rearrange("p (k n) -> p k n", k=8),
               wvf_d.rearrange("(k p) n -> p k n", p=128), tag="dw2")
    t.wait(*mqk)
    t.wait("tre", C.n["tre"])
    for which, wofs, brow, dstb in (("q", 0, rowq, qs_b), ("k", 2048, rowk, ks_b)):
        for k in range(8):
            t.matmul(P_b[0:64, 0:256], SnF[:, k * 64:(k + 1) * 64],
                     wqk[:, wofs + k * 256: wofs + (k + 1) * 256], start=(k == 0), stop=False)
        t.matmul(P_b[0:64, 0:256], cm["ones1_f"][0:1, 0:64], brow[:], start=False, stop=True)
        t.inc("mm")
        v.wait("mm", C.n["mm"])
        v.tensor_copy(dstb[:], P_b[0:64, 0:256])
        v.inc("qke")
        t.wait("qke", C.n["qke"])
    t.wait(*mv)
    for half in range(2):
        for k in range(8):
            t.matmul(P_c[0:64, :], SnF[:, k * 64:(k + 1) * 64],
                     arena[:, k * 1024 + half * 512: k * 1024 + half * 512 + 512],
                     start=(k == 0), stop=False)
        t.matmul(P_c[0:64, :], cm["ones1_f"][0:1, 0:64],
                 rowv[:, half * 512:(half + 1) * 512], start=False, stop=True)
        t.inc("mm")
        v.wait("mm", C.n["mm"])
        v.tensor_copy(vs_t[:, half * 512:(half + 1) * 512], P_c[0:64, :])
        v.inc("vse")
        t.wait("vse", C.n["vse"])
    for src, dstF in ((qs_b, qsF), (ks_b, ksF)):
        for c2 in range(2):
            t.wait("tre", C.n["tr"])
            t.transpose(P_cb[:, 0:64], src[:, c2 * 128:(c2 + 1) * 128], cm["ident_b"][0:64, 0:64])
            t.inc("tr")
            v.wait("tr", C.n["tr"])
            v.tensor_copy(dstF[:, c2 * 64:(c2 + 1) * 64], P_cb[:, 0:64])
            v.inc("tre")
    t.wait("tre", C.n["tre"])
    for c2 in range(2):
        t.matmul(P_a[0:64, 0:64], qsF[:, c2 * 64:(c2 + 1) * 64], ksF[:, c2 * 64:(c2 + 1) * 64],
                 start=(c2 == 0), stop=(c2 == 1))
    t.inc("mm")
    v.wait("mm", C.n["mm"])
    v.tensor_scalar(att_t[:], P_a[0:64, 0:64], 1.0 / 16.0, None, ALU.mult)
    drain(C)
    v.reduce_max(rmx_c[:], att_t[:], axis=AX.X)
    drain(C)
    v.tensor_scalar(rmx_c[:], rmx_c[:], -1.0, None, ALU.mult)
    v.inc("ev")
    s.wait("ev", C.n["ev"])
    s.activation(out=att_t[:], in_=att_t[:], func=AF.Exp, bias=rmx_c[:, 0:1])
    s.inc("sc")
    v.wait("sc", C.n["sc"])
    v.tensor_copy(C.dmb[:], C.dmb[:])
    v.reduce_sum(ssm_c[:], att_t[:], axis=AX.X)
    recip_rows(C, ssm_c[:], ssm_c[:], "dx")
    v.tensor_scalar(attb[:], att_t[:], ssm_c[:, 0:1], None, ALU.mult)
    drain(C)
    t.wait("drv", C.n["drv"])
    t.wait("tre", C.n["tr"])
    t.transpose(P_cb[0:64, 0:64], attb[:], cm["ident_b"][0:64, 0:64])
    t.inc("tr")
    v.wait("tr", C.n["tr"])
    v.tensor_copy(attT[:], P_cb[0:64, 0:64])
    v.inc("tre")
    drain(C)
    v.tensor_copy(si_b[:], vs_t[:])
    v.inc("ev")
    t.wait("ev", C.n["ev"])
    t.wait("tre", C.n["tre"])
    for half in range(2):
        t.matmul(P_a[0:64, :], attT[:], si_b[:, half * 512:(half + 1) * 512],
                 start=True, stop=True)
        t.inc("mm")
        v.wait("mm", C.n["mm"])
        v.tensor_tensor(y_t[:, half * 512:(half + 1) * 512], Sn_t[:, half * 512:(half + 1) * 512],
                        P_a[0:64, :], ALU.add)
        v.inc("ye")
        t.wait("ye", C.n["ye"])
    v.reduce_sum(mu_c[:], y_t[:], axis=AX.X)
    drain(C)
    v.tensor_scalar(mu_c[:], mu_c[:], 1.0 / 1024.0, None, ALU.mult)
    v.inc("ev")
    s.wait("ev", C.n["ev"])
    s.activation(out=gr_t[:], in_=y_t[:], func=AF.Square)
    s.inc("sc")
    v.wait("sc", C.n["sc"])
    v.reduce_sum(var_c[:], gr_t[:], axis=AX.X)
    drain(C)
    v.tensor_scalar(var_c[:], var_c[:], 1.0 / 1024.0, None, ALU.mult)
    v.tensor_tensor(gn_t[0:64, 0:1], mu_c[:], mu_c[:], ALU.mult)
    v.tensor_tensor(var_c[:], var_c[:], gn_t[0:64, 0:1], ALU.subtract)
    v.tensor_scalar(var_c[:], var_c[:], 1e-5, None, ALU.add)
    v.inc("ev")
    s.wait("ev", C.n["ev"])
    s.activation(out=var_c[:], in_=var_c[:], func=AF.Sqrt)
    s.inc("sc")
    v.wait("sc", C.n["sc"])
    recip_rows(C, var_c[:], var_c[:], "dx")
    t.wait("drv", C.n["drv"])
    for half in range(2):
        t.matmul(P_b[0:64, :], cm["ones1_f"][0:1, 0:64], rowg[:, half * 512:(half + 1) * 512],
                 start=True, stop=True)
        t.matmul(P_c[0:64, :], cm["ones1_f"][0:1, 0:64], rowb[:, half * 512:(half + 1) * 512],
                 start=True, stop=True)
        t.inc("mm")
        v.wait("mm", C.n["mm"])
        v.tensor_copy(lngb[:, half * 512:(half + 1) * 512], P_b[0:64, :])
        v.tensor_copy(lnbb[:, half * 512:(half + 1) * 512], P_c[0:64, :])
        v.inc("lge")
        t.wait("lge", C.n["lge"])
    v.tensor_scalar(y_t[:], y_t[:], mu_c[:, 0:1], None, ALU.subtract)
    v.tensor_scalar(y_t[:], y_t[:], var_c[:, 0:1], None, ALU.mult)
    v.tensor_tensor(y_t[:], y_t[:], lngb[:], ALU.mult)
    v.tensor_tensor(y_t[:], y_t[:], lnbb[:], ALU.add)
    drain(C)
    v.tensor_copy(si_b[:], y_t[:])
    v.inc("ev")
    t.wait("ev", C.n["ev"])
    for c8 in range(8):
        t.wait("tre", C.n["tr"])
        t.transpose(P_cb[:, 0:64], si_b[:, c8 * 128:(c8 + 1) * 128], cm["ident_b"][0:64, 0:64])
        t.inc("tr")
        v.wait("tr", C.n["tr"])
        v.tensor_copy(yF[:, c8 * 64:(c8 + 1) * 64], P_cb[:, 0:64])
        v.inc("tre")
    mvp = g.dma(arena[:, 8192:16384].rearrange("p (k n) -> p k n", k=8),
                wvp_d.rearrange("(k p) n -> p k n", p=128), tag="dw2")
    mwo = g.dma(arena[:, 16384:24576].rearrange("p (k n) -> p k n", k=8),
                wof_d.rearrange("(k p) n -> p k n", p=128), tag="dw2")
    t.wait(*mvp)
    t.wait("tre", C.n["tre"])
    for half in range(2):
        for k in range(8):
            t.matmul(P_a[0:64, :], yF[:, k * 64:(k + 1) * 64],
                     arena[:, 8192 + k * 1024 + half * 512: 8192 + k * 1024 + half * 512 + 512],
                     start=(k == 0), stop=False)
        t.matmul(P_a[0:64, :], cm["ones1_f"][0:1, 0:64],
                 rowvp[:, half * 512:(half + 1) * 512], start=False, stop=True)
        t.inc("mm")
        v.wait("mm", C.n["mm"])
        v.tensor_copy(sv_t[:, half * 512:(half + 1) * 512], P_a[0:64, :])
        v.inc("sve")
        t.wait("sve", C.n["sve"])
    drain(C)
    v.tensor_copy(si_b[:], sv_t[:])
    v.inc("ev")
    t.wait("ev", C.n["ev"])
    for c8 in range(8):
        t.wait("tre", C.n["tr"])
        t.transpose(P_cb[:, 0:64], si_b[:, c8 * 128:(c8 + 1) * 128], cm["ident_b"][0:64, 0:64])
        t.inc("tr")
        v.wait("tr", C.n["tr"])
        v.tensor_copy(svF[:, c8 * 64:(c8 + 1) * 64], P_cb[:, 0:64])
        v.inc("tre")
    t.wait(*mwo)
    t.wait("tre", C.n["tre"])
    for half in range(2):
        for k in range(8):
            t.matmul(P_b[0:64, :], svF[:, k * 64:(k + 1) * 64],
                     arena[:, 16384 + k * 1024 + half * 512: 16384 + k * 1024 + half * 512 + 512],
                     start=(k == 0), stop=(k == 7))
        t.inc("mm")
        v.wait("mm", C.n["mm"])
        v.tensor_copy(svwb[:, half * 512:(half + 1) * 512], P_b[0:64, :])
        v.inc("swe")
        t.wait("swe", C.n["swe"])
    for _m in vecs_to_cols(C, [(bof_d, bofc[:])], P_c[:, 0:8]):
        v.wait(*_m)
    import ml_dtypes as _md2
    sel16_np = np.zeros((128, 8 * 16), _md2.bfloat16)
    for c8 in range(8):
        sel16_np[0:64, c8 * 16 + 2 * c8] = 1
        sel16_np[64:128, c8 * 16 + 2 * c8 + 1] = 1
    g.dma(sel16d[:], C.const("k_sel16", sel16_np))
    expd_np = np.zeros((16, 8 * 128), _md2.bfloat16)
    for ch in range(8):
        for hloc in range(2):
            expd_np[2 * ch + hloc, ch * 128 + hloc * 64: ch * 128 + (hloc + 1) * 64] = 1
    g.dma(expdd[:], C.const("k_expd", expd_np))
    C.barrier()

    for tg in range(8):
        lo = tg * TG
        g.dma(qtg[:].rearrange("p (c u) -> p c u", c=8),
              q_d.rearrange("(c p) u -> p c u", p=128)[:, :, lo:lo + TG], tag="dx")
        mx2 = g.dma(x1tg[:].rearrange("p (c u) -> p c u", c=8),
                    x1_d.rearrange("(c p) u -> p c u", p=128)[:, :, lo:lo + TG], tag="dx")
        t.wait(*mx2)
        v.wait(*mx2)
        s.wait(*mx2)
        for hp in range(8):
            hA, hB = 2 * hp, 2 * hp + 1
            for hl, h in ((0, hA), (1, hB)):
                rb_ = (h % 2) * 64
                ch_ = h // 2
                for mc in range(2):
                    ps = (P_a, P_b)[mc]
                    t.wait("fe", C.n["fe"])
                    t.matmul(ps[:], nW0b[rb_:rb_ + 64, h * 256 + mc * 128: h * 256 + (mc + 1) * 128],
                             qtg[rb_:rb_ + 64, ch_ * TG: (ch_ + 1) * TG], start=True, stop=True)
                    t.inc("mm")
                    s.wait("mm", C.n["mm"])
                    s.activation(out=act1rf[:, (hl * 2 + mc) * 512:(hl * 2 + mc + 1) * 512],
                                 in_=ps[:], func=AF.Gelu)
                    s.inc("sc")
                    C.ops["scalar"].append(("wait", C.semname("sc"), C.n["sc"]))
                    v.wait("sc", C.n["sc"])
                    v.tensor_copy(C.dmb[:], C.dmb[:])
                    v.inc("fe")
            t.wait("sc", C.n["sc"])
            t.wait("pe", C.n["pe"])
            for hl, h in ((0, hA), (1, hB)):
                rb_ = (h % 2) * 64
                for hc in range(2):
                    t.matmul(P_c[rb_:rb_ + 64, :],
                             nW1b[:, h * 128 + hc * 64: h * 128 + (hc + 1) * 64],
                             act1rf[:, (hl * 2 + hc) * 512:(hl * 2 + hc + 1) * 512],
                             start=(hc == 0), stop=(hc == 1))
            t.inc("mm")
            v.wait("mm", C.n["mm"])
            v.tensor_copy(prer[:, hp * 512:(hp + 1) * 512], P_c[:])
            v.inc("pe")
        C.barrier()
        base_sc = C.n["sc"]
        for k4 in range(4):
            s.activation(out=sqf[:, k4 * 1024:(k4 + 1) * 1024],
                         in_=prer[:, k4 * 1024:(k4 + 1) * 1024], func=AF.Square)
            s.inc("sc")
        t.wait("sc", base_sc + 4)
        for c8 in range(8):
            t.matmul(P_r[0:16, :], sel16d[:, c8 * 16:(c8 + 1) * 16],
                     sqf[:, c8 * 512:(c8 + 1) * 512], start=(c8 == 0), stop=(c8 == 7))
            t.inc("mm")
        v.wait("mm", C.n["mm"])
        v.tensor_scalar(rqfd[:], P_r[0:16, :], 1.0 / 64.0, 1e-8, ALU.mult, ALU.add)
        v.inc("ev")
        s.wait("ev", C.n["ev"])
        s.activation(out=rqfd[:], in_=rqfd[:], func=AF.Sqrt)
        s.inc("sc")
        v.wait("sc", C.n["sc"])
        recip_rows(C, rqfd[:], rqfd[:], "dx")
        v.tensor_copy(rq16d[:], rqfd[:])
        v.inc("ev")
        t.wait("ev", C.n["ev"])
        for c8 in range(8):
            pr = (P_a, P_b)[c8 % 2]
            if c8 >= 2:
                t.wait("f1e", C.n["f1e"] - 1)
            t.matmul(pr[:], expdd[:, c8 * 128:(c8 + 1) * 128], rq16d[:], start=True, stop=True)
            t.inc("mm")
            pf = (P_c, P_r)[c8 % 2]
            t.matmul(pf[:], svwb[0:64, c8 * 128:(c8 + 1) * 128],
                     afm[0:64, lo:lo + TG], start=True, stop=True)
            t.inc("f1m")
            v.wait("mm", C.n["mm"])
            v.wait("f1m", C.n["f1m"])
            xa = x2acc[:, c8 * 512:(c8 + 1) * 512]
            v.tensor_tensor(xa, prer[:, c8 * 512:(c8 + 1) * 512], pr[:], ALU.mult)
            v.tensor_tensor(xa, xa, qtg[:, c8 * 512:(c8 + 1) * 512], ALU.add)
            v.tensor_tensor(xa, xa, x1tg[:, c8 * 512:(c8 + 1) * 512], ALU.add)
            v.tensor_tensor(xa, xa, pf[:], ALU.add)
            v.tensor_scalar(xa, xa, bofc[:, c8:c8 + 1], None, ALU.add)
            v.inc("f1e")
        y.wait("f1e", C.n["f1e"])
        y.dma(out_o.rearrange("(c p) t -> p c t", p=128)[:, :, lo:lo + TG],
              x2acc[:].rearrange("p (c t) -> p c t", c=8), tag="do")
        C.barrier()
    C.barrier()
    return C


def _collective_ag(C, ins_ap, outs_ap):
    C.ops["gpsimd"].append(("call", "collective_compute", ("AllGather", ALU.bypass),
                            dict(replica_groups=[[0, 1], [2, 3], [4, 5], [6, 7]],
                                 ins=[ins_ap.opt()], outs=[outs_ap.opt()])))
    C.n.setdefault("cc", 0)
    C.ops["gpsimd"].append(("inc", "cc", 1))
    C.n["cc"] += 1
    C.mark["gpsimd"] = ("cc", C.n["cc"])
    C.dirty["gpsimd"] = False
    C.g.wait("cc", C.n["cc"])


def build_mega(race=True):
    C = Ctx("mega", race=race)
    T = 4096
    cm = C.std_consts()
    x0 = C.din("x0", [1024, T], F32)
    ism = C.din("ism", [64, 1], F32)
    W = {}
    for nm, shape in (("wg", [3072, 1024]), ("wu", [3072, 1024]), ("wd", [1024, 3072])):
        W[nm] = C.din(nm, shape, I32)
    for nm, shape in (("Wq", [1024, 1024]), ("Wk", [1024, 1024]), ("Wv", [1024, 1024]),
                      ("lg1", [1024]), ("lb1", [1024]), ("Wo", [1024, 1024]),
                      ("lg2", [1024]), ("lb2", [1024]), ("Wf1", [1024, 4096]),
                      ("Wf2", [4096, 1024]),
                      ("W_K", [1024, 1024]), ("W_V", [1024, 1024]), ("W_Q", [1024, 1024]),
                      ("mW0", [64, 256]), ("mW1", [256, 64]),
                      ("knw", [64]), ("qnw", [64]), ("snw", [1024]), ("rnw", [1024]),
                      ("aw", [1024, 16]), ("tw", [1024, 16]), ("ew", [1024, 16]),
                      ("ab", [16]), ("tb", [16]), ("eb", [16]),
                      ("slot_init", [64, 1024]), ("fw1", [2048, 512]), ("fb1", [512]),
                      ("fw2", [512, 64]), ("fb2", [64]), ("tau", [1]),
                      ("Wih", [3072, 1024]), ("Whh", [3072, 1024]),
                      ("bih", [3072]), ("bhh", [3072]),
                      ("swq", [1024, 256]), ("sbq", [256]), ("swk", [1024, 256]),
                      ("sbk", [256]), ("swv", [1024, 1024]), ("sbv", [1024]),
                      ("lng", [1024]), ("lnb", [1024]), ("wvp", [1024, 1024]),
                      ("bvp", [1024]), ("swo", [1024, 1024]), ("sbo", [1024])):
        W[nm] = C.din(nm, shape, F32)
    out_o = C.dout("out_o", [1024, T], F32)
    xn_i = C.dint("i_xn", [1024, T], BF16)
    r1p_i = C.dint("i_r1p", [1024, T], BF16)
    qf_i = C.dint("i_qf", [1024, T], BF16)
    kf_i = C.dint("i_kf", [1024, T], BF16)
    kt_i = C.dint("i_kt", [T, 1024], BF16)
    vt_i = C.dint("i_vt", [T, 1024], BF16)
    st_i = C.dint("i_st", [64, 16 * 65], F32)
    st_ag = C.dint("i_stag", [128, 16 * 65], F32)
    x1_i = C.dint("i_x1", [1024, T], BF16)
    q_i = C.dint("i_q", [1024, T], BF16)
    al_i = C.dint("i_al", [T, 128], BF16)
    g0_i = C.dint("i_g0", [64, 4096], F32)
    g0_ag = C.dint("i_g0ag", [128, 4096], F32)
    g1_i = C.dint("i_g1", [128, 2048], F32)
    g1_ag = C.dint("i_g1ag", [256, 2048], F32)
    ae_i = C.dint("i_ae", [48, 1], F32)
    ae_ag = C.dint("i_aeag", [96, 1], F32)
    sl_i = C.dint("i_sl", [64, 1025], F32)
    sl_ag = C.dint("i_slag", [128, 1025], F32)

    C.begin_phase("z_")
    zb = C.sbuf("zb", [128, 2048], BF16)
    C.v.memset(zb[:], 0.0)
    C.v.inc("ev")
    C.y.wait("ev", C.n["ev"])
    C.y.dma(al_i.ap()[:, 64:128].rearrange("(a p) n -> p a n", p=128),
            zb[:].rearrange("p (a n) -> p a n", a=32), tag="dz")
    C.barrier()
    C.end_phase()

    C.begin_phase("A_")
    build_a1(C=C, io={"x0": x0, "wg": W["wg"], "wu": W["wu"], "wd": W["wd"],
                      "xn_o": xn_i.ap(), "r_o": r1p_i.ap()})
    C.barrier()
    C.end_phase()

    C.begin_phase("B_")
    build_a2(C=C, io={"xn": xn_i.ap(), "Wq": W["Wq"], "Wk": W["Wk"], "Wv": W["Wv"],
                      "g1": W["lg1"], "b1": W["lb1"],
                      "qf_o": qf_i.ap(), "kf_o": kf_i.ap(), "kt_o": kt_i.ap(),
                      "vt_o": vt_i.ap(), "st_o": st_i.ap()})
    C.barrier()
    C.end_phase()

    _collective_ag(C, st_i.ap(), st_ag.ap())

    C.begin_phase("C_")
    build_b(C=C, io={"xn": xn_i.ap(), "r1p": r1p_i.ap(), "qf": qf_i.ap(),
                     "kf": kf_i.ap(), "kt": kt_i.ap(), "vt": vt_i.ap(),
                     "carry": st_ag.ap()[0:64, :], "ism": ism,
                     "Wo": W["Wo"], "g2": W["lg2"], "b2": W["lb2"],
                     "Wf1": W["Wf1"], "Wf2": W["Wf2"], "x1_o": x1_i.ap()})
    C.barrier()
    C.end_phase()

    C.begin_phase("D_")
    build_c(C=C, io={"x1": x1_i.ap(), "W_K": W["W_K"], "W_V": W["W_V"],
                     "W_Q": W["W_Q"], "mW0": W["mW0"], "mW1": W["mW1"],
                     "knw": W["knw"], "qnw": W["qnw"], "snw": W["snw"],
                     "rnw": W["rnw"], "aw": W["aw"], "tw": W["tw"], "ew": W["ew"],
                     "ab": W["ab"], "tb": W["tb"], "eb": W["eb"],
                     "slot_init": W["slot_init"], "w1": W["fw1"], "b1": W["fb1"],
                     "w2": W["fw2"], "b2": W["fb2"], "tau": W["tau"],
                     "q_o": q_i.ap(), "alpha_o": al_i.ap()[:, 0:64],
                     "g0_o": g0_i.ap(), "g1_o": g1_i.ap(),
                     "slot_o": sl_i.ap(), "ae_o": ae_i.ap()})
    C.barrier()
    C.end_phase()

    _collective_ag(C, g0_i.ap(), g0_ag.ap())
    _collective_ag(C, g1_i.ap(), g1_ag.ap())
    _collective_ag(C, ae_i.ap(), ae_ag.ap())
    _collective_ag(C, sl_i.ap(), sl_ag.ap())

    C.begin_phase("E_")
    build_d2(C=C, io={"x1": x1_i.ap(), "q": q_i.ap(), "af": al_i.ap(),
                     "g0a": g0_ag.ap()[0:64, :], "g0b": g0_ag.ap()[64:128, :],
                     "g1a": g1_ag.ap()[0:128, :], "g1b": g1_ag.ap()[128:256, :],
                     "aea": ae_ag.ap()[0:48, :], "aeb": ae_ag.ap()[48:96, :],
                     "sla": sl_ag.ap()[0:64, :], "slb": sl_ag.ap()[64:128, :],
                     "mW0": W["mW0"], "mW1": W["mW1"], "slot_init": W["slot_init"],
                     "Wih": W["Wih"], "Whh": W["Whh"], "bih": W["bih"],
                     "bhh": W["bhh"], "wq": W["swq"], "bq": W["sbq"],
                     "wk": W["swk"], "bk": W["sbk"], "wv": W["swv"], "bv": W["sbv"],
                     "lng": W["lng"], "lnb": W["lnb"], "wvp": W["wvp"],
                     "bvp": W["bvp"], "wo": W["swo"], "bo": W["sbo"],
                     "out_o": out_o})
    C.barrier()
    C.end_phase()
    return C


from concourse.bass_utils import run_bass_kernel_spmd

_PROGS = {}


def _prog(name, builder):
    if name not in _PROGS:
        _PROGS[name] = builder().build()
    return _PROGS[name]


def kernel(**I):
    x = np.ascontiguousarray(I["L0_attn_x"], dtype=np.float32).reshape(32768, 1024)
    cores = list(range(8))

    def in_map(c):
        return {"x0": np.ascontiguousarray(x[c * 4096:(c + 1) * 4096].T),
                "ism": np.full((64, 1), float(c % 2), np.float32),
                "wg": I["L0_ffn_w_gate"], "wu": I["L0_ffn_w_up"], "wd": I["L0_ffn_w_down"],
                "Wq": I["L0_attn_Wq"], "Wk": I["L0_attn_Wk"], "Wv": I["L0_attn_Wv"],
                "lg1": I["L0_attn_ln1_g"], "lb1": I["L0_attn_ln1_b"],
                "Wo": I["L0_attn_Wo"], "lg2": I["L0_attn_ln2_g"], "lb2": I["L0_attn_ln2_b"],
                "Wf1": I["L0_attn_Wf1"], "Wf2": I["L0_attn_Wf2"],
                "W_K": I["L1_attn_W_K"], "W_V": I["L1_attn_W_V"], "W_Q": I["L1_attn_W_Q"],
                "mW0": I["L1_attn_mem_W0"], "mW1": I["L1_attn_mem_W1"],
                "knw": I["L1_attn_k_norm_w"], "qnw": I["L1_attn_q_norm_w"],
                "snw": I["L1_attn_store_norm_w"], "rnw": I["L1_attn_retrieve_norm_w"],
                "aw": I["L1_attn_alpha_w"], "tw": I["L1_attn_theta_w"], "ew": I["L1_attn_eta_w"],
                "ab": I["L1_attn_alpha_b"], "tb": I["L1_attn_theta_b"], "eb": I["L1_attn_eta_b"],
                "slot_init": I["L1_ffn_slot_init"], "fw1": I["L1_ffn_w1"], "fb1": I["L1_ffn_b1"],
                "fw2": I["L1_ffn_w2"], "fb2": I["L1_ffn_b2"], "tau": I["L1_ffn_tau"],
                "Wih": I["L1_ffn_Wih"], "Whh": I["L1_ffn_Whh"],
                "bih": I["L1_ffn_bih"], "bhh": I["L1_ffn_bhh"],
                "swq": I["L1_ffn_wq"], "sbq": I["L1_ffn_bq"],
                "swk": I["L1_ffn_wk"], "sbk": I["L1_ffn_bk"],
                "swv": I["L1_ffn_wv"], "sbv": I["L1_ffn_bv"],
                "lng": I["L1_ffn_ln_g"], "lnb": I["L1_ffn_ln_b"],
                "wvp": I["L1_ffn_wvp"], "bvp": I["L1_ffn_bvp"],
                "swo": I["L1_ffn_wo"], "sbo": I["L1_ffn_bo"]}

    nc = _prog("mega", build_mega)
    res = run_bass_kernel_spmd(nc, [in_map(c) for c in cores], core_ids=cores)
    out = np.empty((32768, 1024), np.float32)
    for c in cores:
        out[c * 4096:(c + 1) * 4096] = np.asarray(res.results[c]["out_o"], dtype=np.float32).T
    return out.reshape(4, 8192, 1024)
